# version 18
# baseline (speedup 1.0000x reference)
import numpy as np
import concourse.bass as bass
import concourse.mybir as mybir
from concourse.bass_utils import run_bass_kernel_spmd

F32 = mybir.dt.float32
BF16 = mybir.dt.bfloat16
AF = mybir.ActivationFunctionType
ALU = mybir.AluOpType
AX = mybir.AxisListType

T, D, FF = 8192, 1024, 4096
TILE, GRP = 512, 4
NT = T // TILE
PAIRS = 6
WIN = 128
CH = 64
CPT = TILE // CH
NEG = -1.0e30

STAGE_ATTN = True

ENGINES = ("tensor", "vector", "scalar", "gpsimd", "sync")


class Res:
    __slots__ = ("name", "writer", "readers")
    def __init__(self, name):
        self.name = name
        self.writer = None
        self.readers = []

class OpRec:
    __slots__ = ("engine", "fn", "deps", "tag", "dma", "needed", "sem_val",
                 "waits", "idx", "sem_key")
    def __init__(self, engine, fn, deps, tag, dma, idx, sem_key):
        self.engine = engine
        self.fn = fn
        self.deps = deps
        self.tag = tag
        self.dma = dma
        self.needed = False
        self.sem_val = None
        self.waits = []
        self.idx = idx
        self.sem_key = sem_key

class Prog:
    def __init__(self, nc):
        self.nc = nc
        self.ops = {e: [] for e in ENGINES}
        self.all_ops = []
        self.final_deps = []

    def emit(self, engine, fn, reads=(), writes=(), tag="", dma=False, extra_deps=()):
        deps = []
        for r in reads:
            if r.writer is not None:
                deps.append(r.writer)
        for w in writes:
            if w.writer is not None:
                deps.append(w.writer)
            last = {}
            for rd in w.readers:
                key = (rd.engine, rd.dma)
                if key not in last or rd.idx > last[key].idx:
                    last[key] = rd
            deps.extend(last.values())
        for d in extra_deps:
            if d is not None:
                deps.append(d)
        if dma:
            assert len(writes) == 1, f"dma op {tag} must write exactly one Res"
            sem_key = "dr_" + writes[0].name
        else:
            sem_key = "s_" + engine
        op = OpRec(engine, fn, deps, tag, dma, len(self.ops[engine]), sem_key)
        self.ops[engine].append(op)
        self.all_ops.append(op)
        for r in reads:
            r.readers.append(op)
        for w in writes:
            w.writer = op
            w.readers = []
        return op

    def final_wait(self, deps):
        self.final_deps.extend(deps)

    def finalize_block(self, verbose=False):
        nc = self.nc
        for op in self.all_ops:
            for d in op.deps:
                if op.engine == "tensor" and d.engine == "tensor" and not d.dma:
                    continue
                d.needed = True
        for d in self.final_deps:
            d.needed = True
        counters = {}
        sem_of = {}
        for e in ENGINES:
            for op in self.ops[e]:
                key = op.sem_key
                inc = 16 if op.dma else 1
                if op.dma or op.needed:
                    counters[key] = counters.get(key, 0) + inc
                    op.sem_val = counters[key]
                    sem_of[id(op)] = key
        n_waits = 0
        watermark = {e: {} for e in ENGINES}
        for e in ENGINES:
            for op in self.ops[e]:
                best = {}
                for d in op.deps:
                    if op.engine == "tensor" and d.engine == "tensor" and not d.dma:
                        continue
                    key = sem_of[id(d)]
                    if d.sem_val > best.get(key, 0):
                        best[key] = d.sem_val
                wm = watermark[e]
                for key, val in best.items():
                    if val > wm.get(key, 0):
                        wm[key] = val
                        op.waits.append((key, val))
                        n_waits += 1
        final = {}
        for d in self.final_deps:
            key = sem_of[id(d)]
            if d.sem_val > final.get(key, 0):
                final[key] = d.sem_val
        self._final = sorted(final.items())

        sems = {name: nc.alloc_semaphore(name) for name in sorted(set(sem_of.values()))}
        if verbose:
            print("prog:", {e: len(self.ops[e]) for e in ENGINES},
                  "incs:", len(sem_of), "waits:", n_waits)
        with nc.Block() as block:
            for eng_name in ENGINES:
                ops = self.ops[eng_name]
                if not ops and not (eng_name == "sync" and self._final):
                    continue
                section = getattr(block, eng_name)
                def make_body(ops=ops, eng_name=eng_name):
                    def body(eng):
                        for op in ops:
                            for (key, val) in op.waits:
                                eng.wait_ge(sems[key], val)
                            inst = op.fn(eng)
                            if op.sem_val is not None:
                                inst.then_inc(sems[op.sem_key], 16 if op.dma else 1)
                        if eng_name == "sync":
                            for key, val in self._final:
                                eng.wait_ge(sems[key], val)
                    return body
                section(make_body())


def build_program(verbose=False):
    nc = bass.Bass("TRN2", target_bir_lowering=False)

    x_in = nc.dram_tensor("x", [T, D], F32, kind="ExternalInput").ap()
    wq_d = nc.dram_tensor("wq", [D, D], F32, kind="ExternalInput").ap()
    wk_d = nc.dram_tensor("wk", [D, D], F32, kind="ExternalInput").ap()
    wv_d = nc.dram_tensor("wv", [D, D], F32, kind="ExternalInput").ap()
    wo_d = nc.dram_tensor("wo", [D, D], F32, kind="ExternalInput").ap()
    bo_d = nc.dram_tensor("bo", [D], F32, kind="ExternalInput").ap()
    w1_d = nc.dram_tensor("w1", [FF, D], F32, kind="ExternalInput").ap()
    b1_d = nc.dram_tensor("b1", [FF], F32, kind="ExternalInput").ap()
    w2_d = nc.dram_tensor("w2", [D, FF], F32, kind="ExternalInput").ap()
    b2_d = nc.dram_tensor("b2", [D], F32, kind="ExternalInput").ap()
    out_d = nc.dram_tensor("out", [T, D], F32, kind="ExternalOutput").ap()

    scr_w1t = nc.dram_tensor("scr_w1t", [FF, D], BF16, kind="Internal").ap()
    scr_w1T = nc.dram_tensor("scr_w1T", [32, 128, 8, 128], BF16, kind="Internal").ap()
    scr_w2t = nc.dram_tensor("scr_w2t", [D, FF], BF16, kind="Internal").ap()
    scr_w2T = nc.dram_tensor("scr_w2T", [32, 128, D], BF16, kind="Internal").ap()

    S = nc.alloc_sbuf_tensor
    x_tm = [S(f"x_tm{i}", [128, GRP * D], F32) for i in range(2)]
    y_tm = S("y_tm", [128, GRP * D], BF16)
    y_fm = [S(f"y_fm{i}", [128, GRP * D], BF16) for i in range(2)]
    g_sb = S("g_sb", [128, 32 * TILE], BF16)
    w1s = [S(f"w1s{i}", [128, 4096], BF16) for i in range(2)]
    w2s = [S(f"w2s{i}", [128, 2048], BF16) for i in range(3)]
    wfull = [S(f"wfull{i}", [128, 8 * D], BF16) for i in range(2)]
    b1_sb = S("b1_sb", [128, 32], F32)
    b2row = None
    borow = None
    b2bc = S("b2bc", [128, D], BF16)
    bobc = S("bobc", [128, D], BF16)
    onesd = S("onesd", [128, 128], BF16)
    onesr = S("onesr", [1, 128], F32)
    onescol = S("onescol", [128, 1], F32)
    ident = S("ident", [128, 128], BF16)
    eps_ln = S("eps_ln", [128, 1], F32)
    stats = S("stats", [128, 48], F32)
    mv = S("mv", [128, 8], F32)
    rstd = S("rstd", [128, 4], F32)
    sums = S("sums", [128, 32], F32)
    colsum = S("colsum", [128, 1], F32)
    thrT = S("thrT", [1, 2], F32)
    thrv = S("thrv", [128, 2], F32)
    prep_u = None
    prep_v = None
    prep_w = None
    q_loc = S("q_loc", [128, 2 * TILE], BF16)
    k_loc = [S(f"k_loc{i}", [128, 2 * TILE], BF16) for i in range(2)]
    v_loc = [S(f"v_loc{i}", [128, GRP * 256], BF16) for i in range(2)]
    prep_u = q_loc
    prep_v = k_loc[0]
    prep_w = v_loc[0]
    junk1 = v_loc[1]

    qexp = S("qexp", [128, GRP * 768], BF16)
    qpp = S("qpp", [128, PAIRS * TILE], BF16)
    kp_tm = S("kp_tm", [128, GRP * 768], BF16)
    v_lin = S("v_lin", [128, GRP * 768], BF16)
    scr_f32 = S("scr_f32", [128, 1024], F32)
    elu_a = scr_f32
    qf32 = scr_f32
    elu_e = S("elu_e", [128, 768], BF16)
    elu_m = S("elu_m", [128, 768], BF16)
    segmax = S("segmax", [128, 1], F32)
    segsum = S("segsum", [128, 12], F32)
    nrec12 = S("nrec12", [128, 12], F32)
    nraw = S("nraw", [128, 48], F32)
    ncum = S("ncum", [128, 54], F32)
    ncr = S("ncr", [128, 54], F32)
    carry = S("carry", [128, 6], F32)
    ones8 = S("ones8", [128, 8], F32)
    ones2 = S("ones2", [128, 2], BF16)
    statesv = S("statesv", [128, 384], F32)
    cumbf = S("cumbf", [128, 384], BF16)
    cevbf = S("cevbf", [128, 384], BF16)
    attnout = S("attnout", [128, 8 * TILE], BF16)
    masked = scr_f32
    probs = y_tm
    probsT = None
    lmax = S("lmax", [128, 4], F32)
    lsum = S("lsum", [128, 4], F32)
    lrec = S("lrec", [128, 4], F32)
    maskadd = S("maskadd", [128, 256], F32)

    b2row = x_tm[0]
    borow = x_tm[0]
    B2OFF, BOOFF = 0, D
    probsT_off = 1024
    bank = [nc.alloc_psum_tensor(f"bank{i}", [128, 512], F32) for i in range(8)]

    p = Prog(nc)

    r_x = [Res("x0"), Res("x1")]
    r_ytm = Res("ytm")
    r_gc = [Res(f"g{c}") for c in range(32)]
    r_g = Res("g")
    r_yfm = [Res("yfm0"), Res("yfm1")]
    r_w1s = [Res(f"w1s{i}") for i in range(2)]
    r_w2s = [Res(f"w2s{i}") for i in range(3)]
    r_wfull = [Res(f"wfull{i}") for i in range(2)]
    r_bank = [Res(f"bank{i}") for i in range(8)]
    r_const = Res("const")
    r_stats, r_mv, r_rstd = Res("stats"), Res("mv"), Res("rstd")
    r_sums, r_colsum, r_thrT, r_thrv = Res("sums"), Res("colsum"), Res("thrT"), Res("thrv")

    r_scrA1, r_scrT1 = Res("scrA1"), Res("scrT1")
    r_scrA2, r_scrT2 = Res("scrA2"), Res("scrT2")
    r_b2bc, r_bobc = Res("b2bc"), Res("bobc")
    r_qloc = Res("qloc")
    r_kloc = [Res(f"kloc{i}") for i in range(2)]
    r_vloc = [Res(f"vloc{i}") for i in range(2)]
    r_pu, r_pv, r_pw, r_junk = r_qloc, r_kloc[0], r_vloc[0], r_vloc[1]
    r_qexp, r_qpp = Res("qexp"), Res("qpp")
    r_kp, r_vlin = Res("kp"), Res("vlin")
    r_scr32 = Res("scr32")
    r_elua = r_scr32
    r_qf32 = r_scr32
    r_elue, r_elum = Res("elue"), Res("elum")
    r_segmax, r_segsum, r_nrec12 = Res("segmax"), Res("segsum"), Res("nrec12")
    r_nraw, r_ncum, r_ncr, r_carry = Res("nraw"), Res("ncum"), Res("ncr"), Res("carry")
    r_statesv, r_cumbf, r_cevbf = Res("statesv"), Res("cumbf"), Res("cevbf")
    r_attnout = Res("attnout")
    r_masked = r_scr32
    r_probs = r_ytm
    r_probsT = r_ytm
    r_lmax, r_lsum, r_lrec = Res("lmax"), Res("lsum"), Res("lrec")
    r_out = Res("out_dram")

    E = p.emit

    E("vector", lambda e: e.memset(onesr[:], 1.0), writes=[r_const])
    E("vector", lambda e: e.memset(onescol[:], 1.0), writes=[r_const])
    E("vector", lambda e: e.memset(onesd[:], 1.0 / 128.0), writes=[r_const])
    E("vector", lambda e: e.memset(eps_ln[:], 1e-5), writes=[r_const])
    E("vector", lambda e: e.memset(ones8[:], 1.0), writes=[r_const])
    E("vector", lambda e: e.memset(ones2[:], 0.0), writes=[r_const])
    E("vector", lambda e: e.memset(ones2[0:64, 0:1], 1.0), writes=[r_const])
    E("vector", lambda e: e.memset(ones2[64:128, 1:2], 1.0), writes=[r_const])
    E("vector", lambda e: e.memset(carry[:], 0.0), writes=[r_carry])
    E("gpsimd", lambda e: e.memset(ident[:], 0.0), writes=[r_const])
    E("gpsimd", lambda e: e.affine_select(out=ident[:], in_=ident[:],
        compare_op=ALU.not_equal, fill=1.0, base=0, pattern=[[-1, 128]],
        channel_multiplier=1), reads=[r_const], writes=[r_const])
    E("gpsimd", lambda e: e.memset(maskadd[:], 0.0), writes=[r_const])
    E("gpsimd", lambda e: e.affine_select(out=maskadd[:],
        in_=maskadd[:], compare_op=ALU.is_ge, fill=NEG,
        base=WIN, pattern=[[-1, 256]], channel_multiplier=1),
      reads=[r_const], writes=[r_const])
    b1_stage = None
    ident32 = None
    b1_stage = scr_f32
    ident32 = scr_f32
    E("gpsimd", lambda e: e.memset(ident32[0:128, 256:384], 0.0), writes=[r_scr32])
    E("gpsimd", lambda e: e.affine_select(out=ident32[0:128, 256:384],
        in_=ident32[0:128, 256:384],
        compare_op=ALU.not_equal, fill=1.0, base=0, pattern=[[-1, 128]],
        channel_multiplier=1), reads=[r_scr32], writes=[r_scr32])
    E("sync", lambda e: e.dma_start(out=b1_stage[0:32, 0:128],
        in_=b1_d.rearrange("(c p) -> c p", p=128)), writes=[r_scr32], dma=True)
    E("tensor", lambda e: e.transpose(out=bank[5][:, 0:32], in_=b1_stage[0:32, 0:128],
        identity=ident32[0:32, 256:288]), reads=[r_scr32], writes=[r_bank[5]], tag="b1tr")
    E("vector", lambda e: e.tensor_copy(out=b1_sb[:], in_=bank[5][:, 0:32]),
      reads=[r_bank[5]], writes=[r_const])
    E("sync", lambda e: e.dma_start(out=b2row[0:1, B2OFF:B2OFF+D],
        in_=b2_d.unsqueeze(0)), writes=[r_x[0]], dma=True)
    E("sync", lambda e: e.dma_start(out=borow[0:1, BOOFF:BOOFF+D],
        in_=bo_d.unsqueeze(0)), writes=[r_x[0]], dma=True)

    for hh in range(2):
        E("tensor", lambda e, hh=hh: e.matmul(out=bank[6 + hh][:], lhsT=onesr[:],
            rhs=b2row[0:1, B2OFF+hh*512:B2OFF+(hh+1)*512], start=True, stop=True),
          reads=[r_x[0], r_const], writes=[r_bank[6 + hh]], tag="b2mm")
    for hh in range(2):
        E("vector", lambda e, hh=hh: e.tensor_scalar(out=b2bc[:, hh*512:(hh+1)*512],
            in0=bank[6 + hh][:], scalar1=0.5, scalar2=None, op0=ALU.mult),
          reads=[r_bank[6 + hh]], writes=[r_b2bc])
    for hh in range(2):
        E("tensor", lambda e, hh=hh: e.matmul(out=bank[6 + hh][:], lhsT=onesr[:],
            rhs=borow[0:1, BOOFF+hh*512:BOOFF+(hh+1)*512], start=True, stop=True),
          reads=[r_x[0], r_const], writes=[r_bank[6 + hh]], tag="bomm")
    for hh in range(2):
        E("vector", lambda e, hh=hh: e.tensor_copy(out=bobc[:, hh*512:(hh+1)*512],
            in_=bank[6 + hh][:]), reads=[r_bank[6 + hh]], writes=[r_bobc])

    def prep_matrix(w_dram, n_chunks, chunk_f, r_scrA, scrA, scale_half, tag, alt=True):
        per_stage = (GRP * D) // chunk_f
        n_stages = n_chunks // per_stage
        wv_ = w_dram.rearrange("(c p) f -> p c f", p=128)
        def stage_buf(s):
            i = (s % 2) if alt else 1
            return (x_tm[i], r_x[i])
        for s in range(n_stages):
            xb_, rxb_ = stage_buf(s)
            E("sync", lambda e, s=s, xb_=xb_: e.dma_start(
                out=xb_[:].rearrange("p (c f) -> p c f", c=per_stage),
                in_=wv_[:, s*per_stage:(s+1)*per_stage, :]),
              writes=[rxb_], dma=True, tag=f"{tag}ld{s}")
            for cc in range(per_stage):
                c = s * per_stage + cc
                jout, jres = (junk1, r_junk) if chunk_f == 1024 else (y_fm[1], r_yfm[1])
                E("vector",
                  lambda e, cc=cc, c=c, jout=jout, chunk_f=chunk_f, xb_=xb_: e.scalar_tensor_tensor(
                    out=jout[:, 0:chunk_f],
                    in0=xb_[:, cc*chunk_f:(cc+1)*chunk_f],
                    scalar=-1.0, in1=xb_[:, cc*chunk_f:(cc+1)*chunk_f],
                    op0=ALU.mult, op1=ALU.max, accum_out=sums[:, c:c+1]),
                  reads=[rxb_], writes=[jres, r_sums], tag=f"{tag}abs{c}")
        E("vector", lambda e, n_chunks=n_chunks: e.tensor_reduce(out=colsum[:],
            in_=sums[:, 0:n_chunks], axis=AX.X, op=ALU.add),
          reads=[r_sums], writes=[r_colsum])
        E("tensor", lambda e: e.matmul(out=bank[0][0:1, 0:1], lhsT=colsum[:],
            rhs=onescol[:], start=True, stop=True),
          reads=[r_colsum, r_const], writes=[r_bank[0]], tag=f"{tag}tot")
        n_el = n_chunks * 128 * chunk_f
        E("vector", lambda e, n_el=n_el: e.tensor_scalar(out=thrT[0:1, 0:1],
            in0=bank[0][0:1, 0:1], scalar1=0.5 / n_el, scalar2=0.5e-8,
            op0=ALU.mult, op1=ALU.max),
          reads=[r_bank[0]], writes=[r_thrT])
        E("vector", lambda e: e.tensor_scalar(out=thrT[0:1, 1:2], in0=thrT[0:1, 0:1],
            scalar1=-1.0, scalar2=None, op0=ALU.mult),
          reads=[r_thrT], writes=[r_thrT])
        E("tensor", lambda e: e.matmul(out=bank[0][:, 0:2], lhsT=onesr[:],
            rhs=thrT[:], start=True, stop=True),
          reads=[r_thrT, r_const], writes=[r_bank[0]], tag=f"{tag}bc")
        E("vector", lambda e: e.tensor_copy(out=thrv[:], in_=bank[0][:, 0:2]),
          reads=[r_bank[0]], writes=[r_thrv])
        val = 0.5 if scale_half else 1.0
        for s in range(n_stages):
            xb_, rxb_ = stage_buf(s)
            E("sync", lambda e, s=s, xb_=xb_: e.dma_start(
                out=xb_[:].rearrange("p (c f) -> p c f", c=per_stage),
                in_=wv_[:, s*per_stage:(s+1)*per_stage, :]),
              writes=[rxb_], dma=True, tag=f"{tag}ld2_{s}")
            for cc in range(per_stage):
                c = s * per_stage + cc
                if chunk_f == 1024:
                    if c % 2 == 0:
                        ut, vt, wt = q_loc[:, 0:1024], k_loc[0][:, 0:1024], v_loc[0][:, 0:1024]
                        ru, rv, rw = r_qloc, r_kloc[0], r_vloc[0]
                    else:
                        ut, vt, wt = k_loc[1][:, 0:1024], v_loc[1][:, 0:1024], qpp[:, 0:1024]
                        ru, rv, rw = r_kloc[1], r_vloc[1], r_qpp
                else:
                    ut, vt, wt = g_sb[:, 0:4096], g_sb[:, 4096:8192], g_sb[:, 8192:12288]
                    ru = rv = rw = r_g
                E("vector", lambda e, cc=cc, ut=ut, chunk_f=chunk_f, xb_=xb_: e.tensor_scalar(
                    out=ut, in0=xb_[:, cc*chunk_f:(cc+1)*chunk_f],
                    scalar1=thrv[:, 0:1], scalar2=val, op0=ALU.is_gt, op1=ALU.mult),
                  reads=[rxb_, r_thrv], writes=[ru], tag=f"{tag}u{c}")
                E("vector", lambda e, cc=cc, vt=vt, chunk_f=chunk_f, xb_=xb_: e.tensor_scalar(
                    out=vt, in0=xb_[:, cc*chunk_f:(cc+1)*chunk_f],
                    scalar1=thrv[:, 1:2], scalar2=val, op0=ALU.is_lt, op1=ALU.mult),
                  reads=[rxb_, r_thrv], writes=[rv], tag=f"{tag}v{c}")
                E("vector", lambda e, ut=ut, vt=vt, wt=wt: e.tensor_tensor(
                    out=wt, in0=ut, in1=vt, op=ALU.subtract),
                  reads=[ru, rv], writes=[rw], tag=f"{tag}t{c}")
                if chunk_f == 1024:
                    tb = 2 + (c % 2)
                    tbbf = bank[tb][:].bitcast(BF16)
                    for k in range(8):
                        E("tensor", lambda e, k=k, wt=wt, tbbf=tbbf: e.transpose(
                            out=tbbf[:, k*128:(k+1)*128],
                            in_=wt[:, k*128:(k+1)*128], identity=ident[:]),
                          reads=[rw, r_const], writes=[r_bank[tb]], tag=f"{tag}tr{c}")
                    E("scalar", lambda e, c=c, tbbf=tbbf: e.copy(
                        out=kp_tm[:, (c % 2)*1024:(c % 2)*1024 + 1024], in_=tbbf[:]),
                      reads=[r_bank[tb]], writes=[r_kp], tag=f"{tag}ev{c}")
                    if c % 2 == 1:
                        E("scalar", lambda e, c=c: e.dma_start(
                            out=scr_w1T[c-1:c+1].rearrange("s p k f -> p s (k f)"),
                            in_=kp_tm[:, 0:2048].rearrange("p (s f) -> p s f", s=2)),
                          reads=[r_kp], writes=[r_scrT1], dma=True, tag=f"{tag}sl{c}")
                else:
                    for q4 in range(4):
                        tb = 2 + (q4 % 2)
                        tbbf = bank[tb][:].bitcast(BF16)
                        for k in range(8):
                            j = q4 * 8 + k
                            E("tensor", lambda e, j=j, k=k, wt=wt, tbbf=tbbf: e.transpose(
                                out=tbbf[:, k*128:(k+1)*128],
                                in_=wt[:, j*128:(j+1)*128], identity=ident[:]),
                              reads=[rw, r_const], writes=[r_bank[tb]], tag=f"{tag}tr{c}_{j}")
                        E("scalar", lambda e, q4=q4, tbbf=tbbf: e.copy(
                            out=g_sb[:, 12288 + q4*1024: 12288 + (q4+1)*1024], in_=tbbf[:]),
                          reads=[r_bank[tb]], writes=[r_gc[2]], tag=f"{tag}ev{c}_{q4}")
                    for q4 in range(4):
                        E("scalar", lambda e, c=c, q4=q4: e.dma_start(
                            out=scr_w2T[q4*8:(q4+1)*8, :, c*128:(c+1)*128]
                                .rearrange("fc p d -> p fc d"),
                            in_=g_sb[:, 12288 + q4*1024: 12288 + (q4+1)*1024]
                                .rearrange("p (fc d) -> p fc d", fc=8)),
                          reads=[r_gc[2], r_g], writes=[r_scrT2], dma=True, tag=f"{tag}st{c}_{q4}")

    prep_matrix(w1_d, 32, 1024, r_scrA1, scr_w1t, False, "w1")

    x_view = x_in.rearrange("(n g p) d -> n p g d", p=128, g=GRP)
    out_view = out_d.rearrange("(n g p) d -> n p g d", p=128, g=GRP)

    def load_x(t):
        E("sync", lambda e, t=t: e.dma_start(
            out=x_tm[t % 2][:].rearrange("p (g d) -> p g d", g=GRP),
            in_=x_view[t]), writes=[r_x[t % 2]], dma=True, tag=f"ldx{t}")

    def store_x(t):
        op = E("sync", lambda e, t=t: e.dma_start(
            out=out_view[t],
            in_=x_tm[t % 2][:].rearrange("p (g d) -> p g d", g=GRP)),
            reads=[r_x[t % 2]], writes=[r_out], dma=True, tag=f"stx{t}")
        p.final_wait([op])

    def emit_ln_stats_groups(rx, xt, groups):
        for g in groups:
            for hh in range(2):
                E("vector", lambda e, g=g, hh=hh: e.bn_stats(
                    out=stats[:, (g*2+hh)*6:(g*2+hh)*6+6],
                    in_=xt[:, g*D + hh*512: g*D + (hh+1)*512]),
                  reads=[rx], writes=[r_stats], tag="bnst")
            E("vector", lambda e, g=g: e.bn_aggr(out=mv[:, 2*g:2*g+2],
                in_=stats[:, g*12:(g+1)*12]),
              reads=[r_stats], writes=[r_mv], tag="bnag")

    def emit_ln_finalize(rx, xt):
        mview = mv[:].rearrange("p (g two) -> p g two", two=2)
        E("scalar", lambda e, mview=mview: e.activation(out=rstd[:].unsqueeze(2),
            in_=mview[:, :, 1:2], func=AF.Ln, bias=eps_ln[:, 0:1], scale=1.0),
          reads=[r_mv, r_const], writes=[r_rstd], tag="ln_ln")
        E("scalar", lambda e: e.activation(out=rstd[:], in_=rstd[:],
            func=AF.Exp, scale=-0.5), reads=[r_rstd], writes=[r_rstd], tag="ln_exp")
        for g in range(GRP):
            E("vector", lambda e, g=g: e.tensor_scalar(
                out=y_tm[:, g*D:(g+1)*D], in0=xt[:, g*D:(g+1)*D],
                scalar1=mv[:, 2*g:2*g+1], scalar2=rstd[:, g:g+1],
                op0=ALU.subtract, op1=ALU.mult),
              reads=[rx, r_mv, r_rstd], writes=[r_ytm], tag="lnap")

    def emit_ln_stats(rx, xt, yp):
        emit_ln_stats_groups(rx, xt, range(GRP))
        emit_ln_finalize(rx, xt)

    def emit_ln_transposes(yp):
        for g in range(GRP):
            b = g % 2
            bbf = bank[b][:].bitcast(BF16)
            for c in range(8):
                E("tensor", lambda e, g=g, c=c, bbf=bbf: e.transpose(
                    out=bbf[:, c*128:(c+1)*128],
                    in_=y_tm[:, g*D + c*128: g*D + (c+1)*128], identity=ident[:]),
                  reads=[r_ytm, r_const], writes=[r_bank[b]], tag="ytr")
            E("vector", lambda e, g=g, bbf=bbf, yp=yp: e.tensor_copy(
                out=y_fm[yp][:].rearrange("p (c t) -> p c t", c=8)[:, :, g*128:(g+1)*128],
                in_=bbf[:].rearrange("p (c t) -> p c t", c=8)),
              reads=[r_bank[b]], writes=[r_yfm[yp]], tag="ytre")

    def emit_ln(rx, xt, yp):
        emit_ln_stats(rx, xt, yp)
        emit_ln_transposes(yp)

    ffn_seeded = [False]

    def w1_slab_dma(q_):
        E("sync", lambda e, q_=q_: e.dma_start(
            out=w1s[q_ % 2][:].rearrange("p (s k f) -> p s k f", s=4, k=8),
            in_=scr_w1T[4*q_:4*q_+4].rearrange("s p k f -> p s k f")),
          reads=[r_scrT1], writes=[r_w1s[q_ % 2]], dma=True, tag=f"w1sl{q_}")

    def w2_pair_dma(pr_, buf):
        E("scalar", lambda e, pr_=pr_, buf=buf: e.dma_start(
            out=w2s[buf][:].rearrange("p (s d) -> p s d", s=2),
            in_=scr_w2T[2*pr_:2*pr_+2].rearrange("s p d -> p s d")),
          reads=[r_scrT2], writes=[r_w2s[buf]], dma=True, tag=f"w2ch{pr_}")

    def emit_ffn_seed():
        if not ffn_seeded[0]:
            w1_slab_dma(0)
            w2_pair_dma(0, 0)
            w2_pair_dma(1, 1)
            ffn_seeded[0] = True

    def emit_ffn(t, yp, next_ln=None):
        rx, xt = r_x[t % 2], x_tm[t % 2]

        def w1_round(f_c):
            b = 6 + f_c % 2
            buf, sl = (f_c // 4) % 2, f_c % 4
            for k in range(8):
                E("tensor", lambda e, k=k, b=b, buf=buf, sl=sl, yp=yp: e.matmul(
                    out=bank[b][:], lhsT=w1s[buf][:, sl*1024 + k*128: sl*1024 + (k+1)*128],
                    rhs=y_fm[yp][:, k*512:(k+1)*512], start=(k == 0), stop=(k == 7)),
                  reads=[r_w1s[buf], r_yfm[yp]], writes=[r_bank[b]], tag="w1mm")
            if f_c % 4 == 3 and f_c // 4 + 2 < 8:
                w1_slab_dma(f_c // 4 + 2)
            E("scalar", lambda e, f_c=f_c, b=b: e.activation(
                out=g_sb[:, f_c*512:(f_c+1)*512], in_=bank[b][:], func=AF.Gelu,
                bias=b1_sb[:, f_c:f_c+1], scale=1.0),
              reads=[r_bank[b], r_const], writes=[r_gc[f_c], r_g], tag="gelu")

        def w2_round(f_c, grps, buf=None):
            sl = f_c % 2
            if buf is None:
                buf = (f_c // 2) % 3
            for grp in grps:
                gb = [0, 2, 4, 6][grp]
                for hh in range(2):
                    E("tensor", lambda e, f_c=f_c, grp=grp, hh=hh, buf=buf, sl=sl, gb=gb: e.matmul(
                        out=bank[gb+hh][:],
                        lhsT=g_sb[:, f_c*512 + grp*128: f_c*512 + (grp+1)*128],
                        rhs=w2s[buf][:, sl*1024 + hh*512: sl*1024 + (hh+1)*512],
                        start=(f_c == 0), stop=False),
                      reads=[r_gc[f_c], r_w2s[buf]], writes=[r_bank[gb+hh]], tag="w2mm")
            if f_c % 2 == 1 and f_c // 2 + 3 < 16:
                w2_pair_dma(f_c // 2 + 3, (f_c // 2 + 3) % 3)

        def b2add_evac(grp):
            gb = [0, 2, 4, 6][grp]
            for hh in range(2):
                E("tensor", lambda e, hh=hh, gb=gb: e.matmul(
                    out=bank[gb+hh][:], lhsT=onesd[:],
                    rhs=b2bc[:, hh*512:(hh+1)*512], start=False, stop=True),
                  reads=[r_const, r_b2bc], writes=[r_bank[gb+hh]], tag="b2add")
            for hh in range(2):
                E("vector", lambda e, grp=grp, hh=hh, gb=gb: e.tensor_tensor(
                    out=xt[:, grp*D + hh*512: grp*D + (hh+1)*512],
                    in0=bank[gb+hh][:],
                    in1=xt[:, grp*D + hh*512: grp*D + (hh+1)*512], op=ALU.add),
                  reads=[r_bank[gb+hh]], writes=[rx], tag="ffnres")

        emit_ffn_seed()
        w1_slab_dma(1)
        w2_pair_dma(2, 2)
        w1_round(0)
        for f_c in range(1, 32):
            w1_round(f_c)
            w2_round(f_c - 1, (0, 1, 2))
        w2_round(31, (0, 1, 2))
        for grp in (0, 1, 2):
            b2add_evac(grp)
        if next_ln is not None:
            emit_ln_stats_groups(next_ln[0], next_ln[1], (0, 1, 2))
        for pr_ in range(3):
            w2_pair_dma(pr_, pr_ % 3)
        for f_c in range(32):
            w2_round(f_c, (3,))
        b2add_evac(3)
        ffn_seeded[0] = False
        if next_ln is not None:
            emit_ln_stats_groups(next_ln[0], next_ln[1], (3,))
            emit_ln_finalize(next_ln[0], next_ln[1])
            emit_ln_transposes(next_ln[2])

    def emit_attn(t, yp):
        rx, xt = r_x[t % 2], x_tm[t % 2]
        kb = t % 2

        def w_dma(buf, wd, tag):
            E("gpsimd", lambda e, buf=buf, wd=wd: e.dma_start(
                out=wfull[buf][:].rearrange("p (c d) -> p c d", c=8),
                in_=wd.rearrange("(c p) d -> p c d", p=128)),
              writes=[r_wfull[buf]], dma=True, tag=tag)

        w_dma(0, wq_d, f"wq{t}")
        w_dma(1, wk_d, f"wk{t}")

        for j in range(2):
            for d_c in range(8):
                E("tensor", lambda e, j=j, d_c=d_c, yp=yp: e.matmul(
                    out=bank[j][:], lhsT=wfull[0][:, d_c*D + j*128: d_c*D + (j+1)*128],
                    rhs=y_fm[yp][:, d_c*512:(d_c+1)*512], start=(d_c == 0), stop=(d_c == 7)),
                  reads=[r_wfull[0], r_yfm[yp]], writes=[r_bank[j]], tag="qlocmm")
            E("scalar", lambda e, j=j: e.activation(
                out=q_loc[:, j*512:(j+1)*512], in_=bank[j][:], func=AF.Copy,
                scale=0.125), reads=[r_bank[j]], writes=[r_qloc], tag="qloce")
        for grp in range(GRP):
            for hh in range(2):
                n0, n1 = (0, 512) if hh == 0 else (512, 768)
                for d_c in range(8):
                    E("tensor", lambda e, grp=grp, d_c=d_c, hh=hh, n0=n0, n1=n1, yp=yp: e.matmul(
                        out=bank[2+hh][:, 0:n1-n0],
                        lhsT=y_fm[yp][:, d_c*512 + grp*128: d_c*512 + (grp+1)*128],
                        rhs=wfull[0][:, d_c*D + 256 + n0: d_c*D + 256 + n1],
                        start=(d_c == 0), stop=(d_c == 7)),
                      reads=[r_yfm[yp], r_wfull[0]], writes=[r_bank[2+hh]], tag="qlinmm")
            E("scalar", lambda e: e.copy(out=qf32[:, 0:512], in_=bank[2][:]),
              reads=[r_bank[2]], writes=[r_qf32], tag="qle1")
            E("scalar", lambda e: e.copy(out=qf32[:, 512:768],
                in_=bank[3][:, 0:256]), reads=[r_bank[3]], writes=[r_qf32], tag="qle2")
            E("vector", lambda e: e.tensor_reduce(out=segmax[:], in_=qf32[:, 0:768],
                axis=AX.X, op=ALU.max, negate=True),
              reads=[r_qf32], writes=[r_segmax], tag="qmax")
            E("scalar", lambda e, grp=grp: e.activation(
                out=qexp[:, grp*768:(grp+1)*768], in_=qf32[:, 0:768], func=AF.Exp,
                bias=segmax[:, 0:1], scale=1.0),
              reads=[r_qf32, r_segmax], writes=[r_qexp], tag="qexp")
            E("vector", lambda e, grp=grp: e.tensor_reduce(
                out=segsum[:],
                in_=qexp[:, grp*768:(grp+1)*768].rearrange("p (s x) -> p s x", s=12),
                axis=AX.X, op=ALU.add),
              reads=[r_qexp], writes=[r_segsum], tag="qsum")
            E("vector", lambda e: e.reciprocal(out=nrec12[:], in_=segsum[:]),
              reads=[r_segsum], writes=[r_nrec12], tag="qrec")
            E("vector", lambda e, grp=grp: e.tensor_tensor(
                out=qexp[:, grp*768:(grp+1)*768].rearrange("p (s x) -> p s x", s=12),
                in0=qexp[:, grp*768:(grp+1)*768].rearrange("p (s x) -> p s x", s=12),
                in1=nrec12[:].unsqueeze(2).broadcast_to((128, 12, 64)),
                op=ALU.mult),
              reads=[r_qexp, r_nrec12], writes=[r_qexp], tag="qnrm")
        w_dma(0, wv_d, f"wv{t}")
        for grp in range(GRP):
            b = 4 + grp % 2
            bbf = bank[b][:].bitcast(BF16)
            for pr in range(PAIRS):
                E("tensor", lambda e, grp=grp, pr=pr, bbf=bbf: e.transpose(
                    out=bbf[:, pr*128:(pr+1)*128],
                    in_=qexp[:, grp*768 + pr*128: grp*768 + (pr+1)*128],
                    identity=ident[:]),
                  reads=[r_qexp, r_const], writes=[r_bank[b]], tag="qtr")
            E("vector", lambda e, grp=grp, bbf=bbf: e.tensor_copy(
                out=qpp[:].rearrange("p (pr t) -> p pr t", pr=PAIRS)[:, :, grp*128:(grp+1)*128],
                in_=bbf[:, 0:768].rearrange("p (pr t) -> p pr t", pr=PAIRS)),
              reads=[r_bank[b]], writes=[r_qpp], tag="qtre")

        for j in range(2):
            for d_c in range(8):
                E("tensor", lambda e, j=j, d_c=d_c, yp=yp: e.matmul(
                    out=bank[j][:], lhsT=wfull[1][:, d_c*D + j*128: d_c*D + (j+1)*128],
                    rhs=y_fm[yp][:, d_c*512:(d_c+1)*512], start=(d_c == 0), stop=(d_c == 7)),
                  reads=[r_wfull[1], r_yfm[yp]], writes=[r_bank[j]], tag="klocmm")
            E("scalar", lambda e, j=j: e.activation(
                out=k_loc[kb][:, j*512:(j+1)*512], in_=bank[j][:], func=AF.Copy),
              reads=[r_bank[j]], writes=[r_kloc[kb]], tag="kloce")
        for grp in range(GRP):
            for hh in range(2):
                n0, n1 = (0, 512) if hh == 0 else (512, 768)
                for d_c in range(8):
                    E("tensor", lambda e, grp=grp, d_c=d_c, hh=hh, n0=n0, n1=n1, yp=yp: e.matmul(
                        out=bank[2+hh][:, 0:n1-n0],
                        lhsT=y_fm[yp][:, d_c*512 + grp*128: d_c*512 + (grp+1)*128],
                        rhs=wfull[1][:, d_c*D + 256 + n0: d_c*D + 256 + n1],
                        start=(d_c == 0), stop=(d_c == 7)),
                      reads=[r_yfm[yp], r_wfull[1]], writes=[r_bank[2+hh]], tag="klinmm")
            for hh in range(2):
                n0, n1 = (0, 512) if hh == 0 else (512, 768)
                E("vector", lambda e, hh=hh, n0=n0, n1=n1: e.tensor_scalar(
                    out=elu_a[:, n0:n1], in0=bank[2+hh][:, 0:n1-n0],
                    scalar1=0.0, scalar2=None, op0=ALU.min),
                  reads=[r_bank[2+hh]], writes=[r_elua], tag="elmin")
                E("vector", lambda e, hh=hh, n0=n0, n1=n1: e.tensor_scalar(
                    out=elu_m[:, n0:n1], in0=bank[2+hh][:, 0:n1-n0],
                    scalar1=0.0, scalar2=None, op0=ALU.max),
                  reads=[r_bank[2+hh]], writes=[r_elum], tag="elmax")
            E("scalar", lambda e: e.activation(out=elu_e[:], in_=elu_a[:, 0:768],
                func=AF.Exp), reads=[r_elua], writes=[r_elue], tag="elexp")
            E("gpsimd", lambda e, grp=grp: e.tensor_tensor(
                out=kp_tm[:, grp*768:(grp+1)*768], in0=elu_m[:], in1=elu_e[:],
                op=ALU.add), reads=[r_elum, r_elue], writes=[r_kp], tag="eladd")

        w_dma(1, wo_d, f"wo{t}")

        def v_group(grp):
            b0 = 4 + (grp % 2) * 2
            b1b = b0 + 1
            for hh in range(2):
                b = b0 if hh == 0 else b1b
                for d_c in range(8):
                    E("tensor", lambda e, grp=grp, d_c=d_c, hh=hh, b=b, yp=yp: e.matmul(
                        out=bank[b][:],
                        lhsT=y_fm[yp][:, d_c*512 + grp*128: d_c*512 + (grp+1)*128],
                        rhs=wfull[0][:, d_c*D + hh*512: d_c*D + (hh+1)*512],
                        start=(d_c == 0), stop=(d_c == 7)),
                      reads=[r_yfm[yp], r_wfull[0]], writes=[r_bank[b]], tag="vmm")
            E("scalar", lambda e, grp=grp, b0=b0: e.copy(
                out=v_loc[kb][:, grp*256:(grp+1)*256], in_=bank[b0][:, 0:256]),
              reads=[r_bank[b0]], writes=[r_vloc[kb]], tag="vle")
            E("scalar", lambda e, grp=grp, b0=b0: e.copy(
                out=v_lin[:, grp*768: grp*768 + 256], in_=bank[b0][:, 256:512]),
              reads=[r_bank[b0]], writes=[r_vlin], tag="vle2")
            E("scalar", lambda e, grp=grp, b1b=b1b: e.copy(
                out=v_lin[:, grp*768 + 256:(grp+1)*768], in_=bank[b1b][:]),
              reads=[r_bank[b1b]], writes=[r_vlin], tag="vle3")
        v_group(0)
        v_group(1)

        nb = 2
        for grp in range(GRP):
            for pr in range(PAIRS):
                E("tensor", lambda e, grp=grp, pr=pr: e.matmul(
                    out=bank[nb][:, pr*8 + grp*2: pr*8 + grp*2 + 2],
                    lhsT=kp_tm[:, grp*768 + pr*128: grp*768 + (pr+1)*128],
                    rhs=ones2[:], start=(grp == 0 and pr == 0), stop=True,
                    skip_group_check=True),
                  reads=[r_kp, r_const], writes=[r_bank[nb]], tag="nrmm")
        E("vector", lambda e: e.tensor_copy(out=nraw[:], in_=bank[nb][:, 0:48]),
          reads=[r_bank[nb]], writes=[r_nraw], tag="nre")
        E("vector", lambda e: e.tensor_copy(
            out=ncum[:].rearrange("p (pr n) -> p pr n", pr=6)[:, :, 0:1],
            in_=carry[:].unsqueeze(2)),
          reads=[r_carry], writes=[r_ncum], tag="ncar")
        for pr in range(PAIRS):
            E("vector", lambda e, pr=pr: e.tensor_tensor_scan(
                out=ncum[:, pr*9+1:(pr+1)*9], data0=ones8[:],
                data1=nraw[:, pr*8:(pr+1)*8], initial=carry[:, pr:pr+1],
                op0=ALU.mult, op1=ALU.add),
              reads=[r_nraw, r_carry, r_const], writes=[r_ncum], tag="nscan")
        E("vector", lambda e: e.tensor_copy(
            out=carry[:].unsqueeze(2),
            in_=ncum[:].rearrange("p (pr n) -> p pr n", pr=6)[:, :, 8:9]),
          reads=[r_ncum], writes=[r_carry], tag="ncar2")
        E("vector", lambda e: e.tensor_scalar(out=ncr[:], in0=ncum[:],
            scalar1=1e-6, scalar2=None, op0=ALU.add),
          reads=[r_ncum], writes=[r_ncr], tag="nadd")
        E("vector", lambda e: e.reciprocal(out=ncr[:], in_=ncr[:]),
          reads=[r_ncr], writes=[r_ncr], tag="nrec")
        E("vector", lambda e: e.tensor_tensor(
            out=qpp[:].rearrange("p (pr u t) -> p pr u t", pr=PAIRS, u=CPT),
            in0=qpp[:].rearrange("p (pr u t) -> p pr u t", pr=PAIRS, u=CPT),
            in1=ncr[:].rearrange("p (pr n) -> p pr n", pr=PAIRS)[:, :, 0:8]
                .unsqueeze(3).broadcast_to((128, PAIRS, CPT, CH)),
            op=ALU.mult),
          reads=[r_qpp, r_ncr], writes=[r_qpp], tag="qsc")

        mk4 = masked[:].rearrange("p (blk two x) -> p blk two x", two=2, x=256)
        sb_, ceb, aoE, aoO = 3, 4, 5, 6

        def local_bucket(u):
            first_global = (t == 0 and u == 0)
            for h in range(4):
                b = h % 2
                pb = (h % 2) * 64
                col = (h // 2) * 256
                st = (h < 2)
                qsl = q_loc[pb:pb+64, (h//2)*512 + u*128: (h//2)*512 + (u+1)*128]
                if first_global:
                    E("tensor", lambda e, h=h, b=b, pb=pb, col=col, st=st, qsl=qsl: e.matmul(
                        out=bank[b][:, col + 128: col + 256], lhsT=qsl,
                        rhs=k_loc[kb][pb:pb+64, (h//2)*512: (h//2)*512 + 128],
                        start=st, stop=True, skip_group_check=True),
                      reads=[r_qloc, r_kloc[kb]], writes=[r_bank[b]], tag="dots0")
                elif u == 0:
                    E("tensor", lambda e, h=h, b=b, pb=pb, col=col, st=st, qsl=qsl: e.matmul(
                        out=bank[b][:, col: col + 128], lhsT=qsl,
                        rhs=k_loc[1-kb][pb:pb+64, (h//2)*512 + 384: (h//2)*512 + 512],
                        start=st, stop=False, skip_group_check=True),
                      reads=[r_qloc, r_kloc[1-kb]], writes=[r_bank[b]], tag="dotsp")
                    E("tensor", lambda e, h=h, b=b, pb=pb, col=col, qsl=qsl: e.matmul(
                        out=bank[b][:, col + 128: col + 256], lhsT=qsl,
                        rhs=k_loc[kb][pb:pb+64, (h//2)*512: (h//2)*512 + 128],
                        start=False, stop=True, skip_group_check=True),
                      reads=[r_qloc, r_kloc[kb]], writes=[r_bank[b]], tag="dotsc")
                else:
                    E("tensor", lambda e, h=h, u=u, b=b, pb=pb, col=col, st=st, qsl=qsl: e.matmul(
                        out=bank[b][:, col: col + 256], lhsT=qsl,
                        rhs=k_loc[kb][pb:pb+64, (h//2)*512 + (u-1)*128: (h//2)*512 + (u+1)*128],
                        start=st, stop=True, skip_group_check=True),
                      reads=[r_qloc, r_kloc[kb]], writes=[r_bank[b]], tag="dots")
            for par in range(2):
                if first_global:
                    E("vector", lambda e, par=par: e.tensor_tensor(
                        out=mk4[:, :, par, 128:256],
                        in0=bank[par][:].rearrange("p (bb x) -> p bb x", bb=2)[:, :, 128:256],
                        in1=maskadd[:, 128:256].unsqueeze(1).broadcast_to((128, 2, 128)),
                        op=ALU.add),
                      reads=[r_bank[par], r_const], writes=[r_masked], tag="lmsk0")
                else:
                    E("vector", lambda e, par=par: e.tensor_tensor(
                        out=mk4[:, :, par, :],
                        in0=bank[par][:].rearrange("p (bb x) -> p bb x", bb=2),
                        in1=maskadd[:].unsqueeze(1).broadcast_to((128, 2, 256)),
                        op=ALU.add),
                      reads=[r_bank[par], r_const], writes=[r_masked], tag="lmsk")
            vview = (masked[:].rearrange("p (hh x) -> p hh x", hh=4)[:, :, 128:256]
                     if first_global else
                     masked[:].rearrange("p (hh x) -> p hh x", hh=4))
            E("vector", lambda e, vview=vview: e.tensor_reduce(
                out=lmax[:], in_=vview, axis=AX.X, op=ALU.max, negate=True),
              reads=[r_masked], writes=[r_lmax], tag="lmx")
            for h in range(4):
                sl = (slice(h*256 + 128, (h+1)*256) if first_global
                      else slice(h*256, (h+1)*256))
                E("scalar", lambda e, h=h, sl=sl: e.activation(
                    out=probs[:, sl], in_=masked[:, sl], func=AF.Exp,
                    bias=lmax[:, h:h+1], scale=1.0, accum_out=lsum[:, h:h+1]),
                  reads=[r_masked, r_lmax], writes=[r_probs, r_lsum], tag="lexp")
            E("vector", lambda e: e.reciprocal(out=lrec[:], in_=lsum[:]),
              reads=[r_lsum], writes=[r_lrec], tag="lrc")
            E("gpsimd", lambda e: e.tensor_tensor(
                out=probs[:, 0:1024].rearrange("p (h x) -> p h x", h=4),
                in0=probs[:, 0:1024].rearrange("p (h x) -> p h x", h=4),
                in1=lrec[:].unsqueeze(2).broadcast_to((128, 4, 256)),
                op=ALU.mult),
              reads=[r_probs, r_lrec], writes=[r_probs], tag="lnm")
            tbbf = bank[2][:].bitcast(BF16)
            if first_global:
                E("vector", lambda e: e.memset(bank[2][:], 0.0),
                  writes=[r_bank[2]], tag="ptz")
            for h in range(4):
                for kc in range(2):
                    if first_global and kc == 0:
                        continue
                    E("tensor", lambda e, h=h, kc=kc, tbbf=tbbf: e.transpose(
                        out=tbbf[:, (h*2+kc)*128:(h*2+kc+1)*128],
                        in_=probs[:, h*256 + kc*128: h*256 + (kc+1)*128],
                        identity=ident[:]),
                      reads=[r_probs, r_const], writes=[r_bank[2]], tag="ptr")
            E("vector", lambda e, tbbf=tbbf: e.tensor_copy(
                out=probs[:, 1024:2048], in_=tbbf[:]),
              reads=[r_bank[2]], writes=[r_probsT], tag="ptre")
            for h in range(4):
                pb = (h % 2) * 64
                acol = (h // 2) * 128
                for kc in range(2):
                    if first_global and kc == 0:
                        continue
                    if u == 0 and kc == 0:
                        vbuf, vcol = 1 - kb, 3 * 256
                    else:
                        vbuf, vcol = kb, (u - 1 + kc) * 256
                    st = ((kc == 0) or first_global) and h < 2
                    E("tensor", lambda e, h=h, kc=kc, pb=pb, acol=acol, vbuf=vbuf, vcol=vcol, st=st, kcf=(kc == 0 or first_global): e.matmul(
                        out=bank[7][pb:pb+64, acol:acol+128],
                        lhsT=v_loc[vbuf][:, vcol + h*64: vcol + (h+1)*64],
                        rhs=probs[:, 1024 + (h*2+kc)*128: 1024 + (h*2+kc+1)*128],
                        start=st, stop=(kc == 1), skip_group_check=True),
                      reads=[r_vloc[vbuf], r_probsT], writes=[r_bank[7]], tag="avmm")
            E("vector", lambda e, u=u: e.tensor_copy(
                out=attnout[:, u*128:].rearrange("p (j t) -> p j t", j=2)[:, :, 0:128]
                    if False else
                    attnout[:].rearrange("p (j t) -> p j t", j=8)[:, 0:2, u*128:(u+1)*128],
                in_=bank[7][:, 0:256].rearrange("p (j t) -> p j t", j=2)),
              reads=[r_bank[7]], writes=[r_attnout], tag="ave")

        aout6 = attnout[:, 2*512:].rearrange("p (pr t) -> p pr t", pr=6)

        def linear_pair(cp):
            ue, uo = 2*cp, 2*cp + 1
            first = (t == 0 and cp == 0)
            if not first:
                E("scalar", lambda e: e.copy(out=cumbf[:], in_=bank[sb_][:, 0:384]),
                  reads=[r_bank[sb_]], writes=[r_cumbf], tag="cume")
            for h in range(12):
                pb = (h % 2) * 64
                E("tensor", lambda e, h=h, cp=cp, pb=pb: e.matmul(
                    out=bank[ceb][pb:pb+64, (h//2)*64:(h//2)*64+64],
                    lhsT=kp_tm[0:64, cp*768 + h*64: cp*768 + (h+1)*64],
                    rhs=v_lin[0:64, cp*768 + h*64: cp*768 + (h+1)*64],
                    start=(h < 2), stop=True, skip_group_check=True),
                  reads=[r_kp, r_vlin], writes=[r_bank[ceb]], tag="cevmm")
            E("scalar", lambda e: e.copy(out=cevbf[:], in_=bank[ceb][:, 0:384]),
              reads=[r_bank[ceb]], writes=[r_cevbf], tag="ceve")
            if not first:
                for h in range(12):
                    pb = (h % 2) * 64
                    pr = h // 2
                    ab = aoE if h % 2 == 0 else aoO
                    E("tensor", lambda e, h=h, ue=ue, pb=pb, pr=pr, ab=ab: e.matmul(
                        out=bank[ab][pb:pb+64, pr*64:(pr+1)*64],
                        lhsT=cumbf[pb:pb+64, pr*64:(pr+1)*64],
                        rhs=qpp[pb:pb+64, pr*512 + ue*64: pr*512 + (ue+1)*64],
                        start=(h < 2), stop=True, skip_group_check=True),
                      reads=[r_cumbf, r_qpp], writes=[r_bank[ab]], tag="atme")
                for par in range(2):
                    ab = aoE if par == 0 else aoO
                    pb = par * 64
                    E("scalar", lambda e, ue=ue, ab=ab, pb=pb: e.copy(
                        out=aout6[pb:pb+64, :, ue*64:(ue+1)*64],
                        in_=bank[ab][pb:pb+64, 0:384].rearrange("p (pr t) -> p pr t", pr=6)),
                      reads=[r_bank[ab]], writes=[r_attnout], tag="aoe")
            else:
                for pb in (0, 64):
                    E("vector", lambda e, pb=pb: e.memset(
                        aout6[pb:pb+64, :, 0:64], 0.0),
                      writes=[r_attnout], tag="aoz")
            for h in range(12):
                pb = (h % 2) * 64
                pr = h // 2
                ab = aoE if h % 2 == 0 else aoO
                E("tensor", lambda e, h=h, uo=uo, pb=pb, pr=pr, ab=ab, first=first: e.matmul(
                    out=bank[ab][pb:pb+64, pr*64:(pr+1)*64],
                    lhsT=cevbf[pb:pb+64, pr*64:(pr+1)*64],
                    rhs=qpp[pb:pb+64, pr*512 + uo*64: pr*512 + (uo+1)*64],
                    start=(h < 2), stop=first, skip_group_check=True),
                  reads=[r_cevbf, r_qpp], writes=[r_bank[ab]], tag="atmo1")
            if not first:
                for h in range(12):
                    pb = (h % 2) * 64
                    pr = h // 2
                    ab = aoE if h % 2 == 0 else aoO
                    E("tensor", lambda e, h=h, uo=uo, pb=pb, pr=pr, ab=ab: e.matmul(
                        out=bank[ab][pb:pb+64, pr*64:(pr+1)*64],
                        lhsT=cumbf[pb:pb+64, pr*64:(pr+1)*64],
                        rhs=qpp[pb:pb+64, pr*512 + uo*64: pr*512 + (uo+1)*64],
                        start=False, stop=True, skip_group_check=True),
                      reads=[r_cumbf, r_qpp], writes=[r_bank[ab]], tag="atmo2")
            for par in range(2):
                ab = aoE if par == 0 else aoO
                pb = par * 64
                E("scalar", lambda e, uo=uo, ab=ab, pb=pb: e.copy(
                    out=aout6[pb:pb+64, :, uo*64:(uo+1)*64],
                    in_=bank[ab][pb:pb+64, 0:384].rearrange("p (pr t) -> p pr t", pr=6)),
                  reads=[r_bank[ab]], writes=[r_attnout], tag="aoo")
            for h in range(12):
                pb = (h % 2) * 64
                E("tensor", lambda e, h=h, cp=cp, pb=pb, first=first: e.matmul(
                    out=bank[sb_][pb:pb+64, (h//2)*64:(h//2)*64+64],
                    lhsT=kp_tm[:, cp*768 + h*64: cp*768 + (h+1)*64],
                    rhs=v_lin[:, cp*768 + h*64: cp*768 + (h+1)*64],
                    start=(first and h < 2), stop=True, skip_group_check=True),
                  reads=[r_kp, r_vlin], writes=[r_bank[sb_]], tag="stmm")

        if t > 0:
            E("vector", lambda e: e.tensor_copy(out=bank[sb_][:, 0:384],
                in_=statesv[:]), reads=[r_statesv], writes=[r_bank[sb_]], tag="strst")
        local_bucket(0)
        v_group(2)
        linear_pair(0)
        local_bucket(1)
        v_group(3)
        linear_pair(1)
        local_bucket(2)
        linear_pair(2)
        local_bucket(3)
        linear_pair(3)
        E("vector", lambda e: e.tensor_copy(out=statesv[:], in_=bank[sb_][:, 0:384]),
          reads=[r_bank[sb_]], writes=[r_statesv], tag="stsv")

        for grp in range(GRP):
            b0 = 1 if grp % 2 == 0 else 2
            b1b = 6 if grp % 2 == 0 else 7
            for hh in range(2):
                b = b0 if hh == 0 else b1b
                for d_c in range(8):
                    E("tensor", lambda e, grp=grp, d_c=d_c, hh=hh, b=b: e.matmul(
                        out=bank[b][:],
                        lhsT=attnout[:, d_c*512 + grp*128: d_c*512 + (grp+1)*128],
                        rhs=wfull[1][:, d_c*D + hh*512: d_c*D + (hh+1)*512],
                        start=(d_c == 0), stop=False),
                      reads=[r_attnout, r_wfull[1]], writes=[r_bank[b]], tag="womm")
                E("tensor", lambda e, hh=hh, b=b: e.matmul(
                    out=bank[b][:], lhsT=onesd[:], rhs=bobc[:, hh*512:(hh+1)*512],
                    start=False, stop=True),
                  reads=[r_const, r_bobc], writes=[r_bank[b]], tag="boadd")
                E("vector", lambda e, grp=grp, hh=hh, b=b: e.tensor_tensor(
                    out=xt[:, grp*D + hh*512: grp*D + (hh+1)*512], in0=bank[b][:],
                    in1=xt[:, grp*D + hh*512: grp*D + (hh+1)*512], op=ALU.add),
                  reads=[r_bank[b]], writes=[rx], tag="wores")

    load_x(0)
    ln_i = 0
    emit_ln(r_x[0], x_tm[0], ln_i % 2)
    prep_matrix(w2_d, 8, 4096, r_scrA2, scr_w2t, True, "w2", alt=False)

    load_x(1)
    for t in range(NT):
        rx, xt = r_x[t % 2], x_tm[t % 2]
        p1 = ln_i % 2
        if STAGE_ATTN:
            ln_i += 1
            emit_ffn(t, p1, next_ln=(rx, xt, ln_i % 2))
            emit_ffn_seed()
            emit_attn(t, ln_i % 2)
        else:
            emit_ffn(t, p1)
        ln_i += 1
        p3 = ln_i % 2
        emit_ln(rx, xt, p3)
        if t + 1 < NT:
            ln_i += 1
            emit_ln_stats(r_x[(t + 1) % 2], x_tm[(t + 1) % 2], ln_i % 2)
        emit_ffn(t, p3)
        emit_ffn_seed()
        store_x(t)
        if t + 2 < NT:
            load_x(t + 2)
        if t + 1 < NT:
            emit_ln_transposes(ln_i % 2)

    p.finalize_block(verbose=verbose)
    return nc


def kernel(L0_attn_x, L0_attn_Wq, L0_attn_Wk, L0_attn_Wv, L0_attn_Wo, L0_attn_bo,
           L0_ffn_W1, L0_ffn_b1, L0_ffn_W2, L0_ffn_b2):
    nc = build_program()
    xs = np.ascontiguousarray(np.asarray(L0_attn_x, dtype=np.float32))
    w = {
        "wq": np.ascontiguousarray(np.asarray(L0_attn_Wq, np.float32)),
        "wk": np.ascontiguousarray(np.asarray(L0_attn_Wk, np.float32)),
        "wv": np.ascontiguousarray(np.asarray(L0_attn_Wv, np.float32)),
        "wo": np.ascontiguousarray(np.asarray(L0_attn_Wo, np.float32)),
        "bo": np.ascontiguousarray(np.asarray(L0_attn_bo, np.float32)),
        "w1": np.ascontiguousarray(np.asarray(L0_ffn_W1, np.float32)),
        "b1": np.ascontiguousarray(np.asarray(L0_ffn_b1, np.float32)),
        "w2": np.ascontiguousarray(np.asarray(L0_ffn_W2, np.float32)),
        "b2": np.ascontiguousarray(np.asarray(L0_ffn_b2, np.float32)),
    }
    in_maps = [{"x": xs[b], **w} for b in range(8)]
    res = run_bass_kernel_spmd(nc, in_maps, core_ids=list(range(8)))
    return np.stack([res.results[b]["out"] for b in range(8)], axis=0)


# version 19 (final)
# speedup vs baseline: 1.0476x; 1.0476x over previous
import numpy as np
import concourse.bass as bass
import concourse.mybir as mybir
from concourse.bass_utils import run_bass_kernel_spmd

F32 = mybir.dt.float32
BF16 = mybir.dt.bfloat16
AF = mybir.ActivationFunctionType
ALU = mybir.AluOpType
AX = mybir.AxisListType

T, D, FF = 8192, 1024, 4096
TILE, GRP = 512, 4
NT = T // TILE
PAIRS = 6
WIN = 128
CH = 64
CPT = TILE // CH
NEG = -1.0e30

STAGE_ATTN = True

ENGINES = ("tensor", "vector", "scalar", "gpsimd", "sync")


class Res:
    __slots__ = ("name", "writer", "readers")
    def __init__(self, name):
        self.name = name
        self.writer = None
        self.readers = []

class OpRec:
    __slots__ = ("engine", "fn", "deps", "tag", "dma", "needed", "sem_val",
                 "waits", "idx", "sem_key")
    def __init__(self, engine, fn, deps, tag, dma, idx, sem_key):
        self.engine = engine
        self.fn = fn
        self.deps = deps
        self.tag = tag
        self.dma = dma
        self.needed = False
        self.sem_val = None
        self.waits = []
        self.idx = idx
        self.sem_key = sem_key

class Prog:
    def __init__(self, nc):
        self.nc = nc
        self.ops = {e: [] for e in ENGINES}
        self.all_ops = []
        self.final_deps = []

    def emit(self, engine, fn, reads=(), writes=(), tag="", dma=False, extra_deps=()):
        deps = []
        for r in reads:
            if r.writer is not None:
                deps.append(r.writer)
        for w in writes:
            if w.writer is not None:
                deps.append(w.writer)
            last = {}
            for rd in w.readers:
                key = (rd.engine, rd.dma)
                if key not in last or rd.idx > last[key].idx:
                    last[key] = rd
            deps.extend(last.values())
        for d in extra_deps:
            if d is not None:
                deps.append(d)
        if dma:
            assert len(writes) == 1, f"dma op {tag} must write exactly one Res"
            sem_key = "dr_" + writes[0].name
        else:
            sem_key = "s_" + engine
        op = OpRec(engine, fn, deps, tag, dma, len(self.ops[engine]), sem_key)
        self.ops[engine].append(op)
        self.all_ops.append(op)
        for r in reads:
            r.readers.append(op)
        for w in writes:
            w.writer = op
            w.readers = []
        return op

    def final_wait(self, deps):
        self.final_deps.extend(deps)

    def finalize_block(self, verbose=False):
        nc = self.nc
        for op in self.all_ops:
            for d in op.deps:
                if op.engine == "tensor" and d.engine == "tensor" and not d.dma:
                    continue
                d.needed = True
        for d in self.final_deps:
            d.needed = True
        counters = {}
        sem_of = {}
        for e in ENGINES:
            for op in self.ops[e]:
                key = op.sem_key
                inc = 16 if op.dma else 1
                if op.dma or op.needed:
                    counters[key] = counters.get(key, 0) + inc
                    op.sem_val = counters[key]
                    sem_of[id(op)] = key
        n_waits = 0
        watermark = {e: {} for e in ENGINES}
        for e in ENGINES:
            for op in self.ops[e]:
                best = {}
                for d in op.deps:
                    if op.engine == "tensor" and d.engine == "tensor" and not d.dma:
                        continue
                    key = sem_of[id(d)]
                    if d.sem_val > best.get(key, 0):
                        best[key] = d.sem_val
                wm = watermark[e]
                for key, val in best.items():
                    if val > wm.get(key, 0):
                        wm[key] = val
                        op.waits.append((key, val))
                        n_waits += 1
        final = {}
        for d in self.final_deps:
            key = sem_of[id(d)]
            if d.sem_val > final.get(key, 0):
                final[key] = d.sem_val
        self._final = sorted(final.items())

        sems = {name: nc.alloc_semaphore(name) for name in sorted(set(sem_of.values()))}
        if verbose:
            print("prog:", {e: len(self.ops[e]) for e in ENGINES},
                  "incs:", len(sem_of), "waits:", n_waits)
        with nc.Block() as block:
            for eng_name in ENGINES:
                ops = self.ops[eng_name]
                if not ops and not (eng_name == "sync" and self._final):
                    continue
                section = getattr(block, eng_name)
                def make_body(ops=ops, eng_name=eng_name):
                    def body(eng):
                        for op in ops:
                            for (key, val) in op.waits:
                                eng.wait_ge(sems[key], val)
                            inst = op.fn(eng)
                            if op.sem_val is not None:
                                inst.then_inc(sems[op.sem_key], 16 if op.dma else 1)
                        if eng_name == "sync":
                            for key, val in self._final:
                                eng.wait_ge(sems[key], val)
                    return body
                section(make_body())


def build_program(verbose=False):
    nc = bass.Bass("TRN2", target_bir_lowering=False)

    x_in = nc.dram_tensor("x", [T, D], F32, kind="ExternalInput").ap()
    wq_d = nc.dram_tensor("wq", [D, D], F32, kind="ExternalInput").ap()
    wk_d = nc.dram_tensor("wk", [D, D], F32, kind="ExternalInput").ap()
    wv_d = nc.dram_tensor("wv", [D, D], F32, kind="ExternalInput").ap()
    wo_d = nc.dram_tensor("wo", [D, D], F32, kind="ExternalInput").ap()
    bo_d = nc.dram_tensor("bo", [D], F32, kind="ExternalInput").ap()
    w1_d = nc.dram_tensor("w1", [FF, D], F32, kind="ExternalInput").ap()
    b1_d = nc.dram_tensor("b1", [FF], F32, kind="ExternalInput").ap()
    w2_d = nc.dram_tensor("w2", [D, FF], F32, kind="ExternalInput").ap()
    b2_d = nc.dram_tensor("b2", [D], F32, kind="ExternalInput").ap()
    out_d = nc.dram_tensor("out", [T, D], F32, kind="ExternalOutput").ap()

    scr_w1t = nc.dram_tensor("scr_w1t", [FF, D], BF16, kind="Internal").ap()
    scr_w1T = nc.dram_tensor("scr_w1T", [32, 128, 8, 128], BF16, kind="Internal").ap()
    scr_w2t = nc.dram_tensor("scr_w2t", [D, FF], BF16, kind="Internal").ap()
    scr_w2T = nc.dram_tensor("scr_w2T", [32, 128, D], BF16, kind="Internal").ap()

    S = nc.alloc_sbuf_tensor
    x_tm = [S(f"x_tm{i}", [128, GRP * D], F32) for i in range(2)]
    y_tm = S("y_tm", [128, GRP * D], BF16)
    y_fm = [S(f"y_fm{i}", [128, GRP * D], BF16) for i in range(2)]
    g_sb = S("g_sb", [128, 32 * TILE], BF16)
    w1s = [S(f"w1s{i}", [128, 4096], BF16) for i in range(2)]
    w2s = [S(f"w2s{i}", [128, 2048], BF16) for i in range(3)]
    wfull = [S(f"wfull{i}", [128, 8 * D], BF16) for i in range(2)]
    b1_sb = S("b1_sb", [128, 32], F32)
    b2row = None
    borow = None
    b2bc = S("b2bc", [128, D], BF16)
    bobc = S("bobc", [128, D], BF16)
    onesd = S("onesd", [128, 128], BF16)
    onesr = S("onesr", [1, 128], F32)
    onescol = S("onescol", [128, 1], F32)
    ident = S("ident", [128, 128], BF16)
    eps_ln = S("eps_ln", [128, 1], F32)
    stats = S("stats", [128, 48], F32)
    mv = S("mv", [128, 8], F32)
    rstd = S("rstd", [128, 4], F32)
    sums = S("sums", [128, 32], F32)
    colsum = S("colsum", [128, 1], F32)
    thrT = S("thrT", [1, 2], F32)
    thrv = S("thrv", [128, 2], F32)
    prep_u = None
    prep_v = None
    prep_w = None
    q_loc = S("q_loc", [128, 2 * TILE], BF16)
    k_loc = [S(f"k_loc{i}", [128, 2 * TILE], BF16) for i in range(2)]
    v_loc = [S(f"v_loc{i}", [128, GRP * 256], BF16) for i in range(2)]
    prep_u = q_loc
    prep_v = k_loc[0]
    prep_w = v_loc[0]
    junk1 = v_loc[1]

    qexp = S("qexp", [128, GRP * 768], BF16)
    qpp = S("qpp", [128, PAIRS * TILE], BF16)
    kp_tm = S("kp_tm", [128, GRP * 768], BF16)
    v_lin = S("v_lin", [128, GRP * 768], BF16)
    scr_f32 = S("scr_f32", [128, 1024], F32)
    elu_a = scr_f32
    qf32 = scr_f32
    elu_e = S("elu_e", [128, 768], BF16)
    elu_m = S("elu_m", [128, 768], BF16)
    segmax = S("segmax", [128, 1], F32)
    segsum = S("segsum", [128, 12], F32)
    nrec12 = S("nrec12", [128, 12], F32)
    nraw = S("nraw", [128, 48], F32)
    ncum = S("ncum", [128, 54], F32)
    ncr = S("ncr", [128, 54], F32)
    carry = S("carry", [128, 6], F32)
    ones8 = S("ones8", [128, 8], F32)
    ones2 = S("ones2", [128, 2], BF16)
    statesv = S("statesv", [128, 384], F32)
    cumbf = S("cumbf", [128, 384], BF16)
    cevbf = S("cevbf", [128, 384], BF16)
    attnout = S("attnout", [128, 8 * TILE], BF16)
    masked = scr_f32
    probs = y_tm
    probsT = None
    lmax = S("lmax", [128, 4], F32)
    lsum = S("lsum", [128, 4], F32)
    lrec = S("lrec", [128, 4], F32)
    maskadd = S("maskadd", [128, 256], F32)

    b2row = x_tm[0]
    borow = x_tm[0]
    B2OFF, BOOFF = 0, D
    probsT_off = 1024
    bank = [nc.alloc_psum_tensor(f"bank{i}", [128, 512], F32) for i in range(8)]

    p = Prog(nc)

    r_x = [Res("x0"), Res("x1")]
    r_ytm = Res("ytm")
    r_gc = [Res(f"g{c}") for c in range(32)]
    r_g = Res("g")
    r_yfm = [Res("yfm0"), Res("yfm1")]
    r_w1s = [Res(f"w1s{i}") for i in range(2)]
    r_w2s = [Res(f"w2s{i}") for i in range(3)]
    r_wfull = [Res(f"wfull{i}") for i in range(2)]
    r_bank = [Res(f"bank{i}") for i in range(8)]
    r_const = Res("const")
    r_stats, r_mv, r_rstd = Res("stats"), Res("mv"), Res("rstd")
    r_sums, r_colsum, r_thrT, r_thrv = Res("sums"), Res("colsum"), Res("thrT"), Res("thrv")

    r_scrA1, r_scrT1 = Res("scrA1"), Res("scrT1")
    r_scrA2, r_scrT2 = Res("scrA2"), Res("scrT2")
    r_b2bc, r_bobc = Res("b2bc"), Res("bobc")
    r_qloc = Res("qloc")
    r_kloc = [Res(f"kloc{i}") for i in range(2)]
    r_vloc = [Res(f"vloc{i}") for i in range(2)]
    r_pu, r_pv, r_pw, r_junk = r_qloc, r_kloc[0], r_vloc[0], r_vloc[1]
    r_qexp, r_qpp = Res("qexp"), Res("qpp")
    r_kp, r_vlin = Res("kp"), Res("vlin")
    r_scr32 = Res("scr32")
    r_elua = r_scr32
    r_qf32 = r_scr32
    r_elue, r_elum = Res("elue"), Res("elum")
    r_segmax, r_segsum, r_nrec12 = Res("segmax"), Res("segsum"), Res("nrec12")
    r_nraw, r_ncum, r_ncr, r_carry = Res("nraw"), Res("ncum"), Res("ncr"), Res("carry")
    r_statesv, r_cumbf, r_cevbf = Res("statesv"), Res("cumbf"), Res("cevbf")
    r_attnout = Res("attnout")
    r_masked = r_scr32
    r_probs = r_ytm
    r_probsT = r_ytm
    r_lmax, r_lsum, r_lrec = Res("lmax"), Res("lsum"), Res("lrec")
    r_out = Res("out_dram")

    E = p.emit

    E("vector", lambda e: e.memset(onesr[:], 1.0), writes=[r_const])
    E("vector", lambda e: e.memset(onescol[:], 1.0), writes=[r_const])
    E("vector", lambda e: e.memset(onesd[:], 1.0 / 128.0), writes=[r_const])
    E("vector", lambda e: e.memset(eps_ln[:], 1e-5), writes=[r_const])
    E("vector", lambda e: e.memset(ones8[:], 1.0), writes=[r_const])
    E("vector", lambda e: e.memset(ones2[:], 0.0), writes=[r_const])
    E("vector", lambda e: e.memset(ones2[0:64, 0:1], 1.0), writes=[r_const])
    E("vector", lambda e: e.memset(ones2[64:128, 1:2], 1.0), writes=[r_const])
    E("vector", lambda e: e.memset(carry[:], 0.0), writes=[r_carry])
    E("gpsimd", lambda e: e.memset(ident[:], 0.0), writes=[r_const])
    E("gpsimd", lambda e: e.affine_select(out=ident[:], in_=ident[:],
        compare_op=ALU.not_equal, fill=1.0, base=0, pattern=[[-1, 128]],
        channel_multiplier=1), reads=[r_const], writes=[r_const])
    E("gpsimd", lambda e: e.memset(maskadd[:], 0.0), writes=[r_const])
    E("gpsimd", lambda e: e.affine_select(out=maskadd[:],
        in_=maskadd[:], compare_op=ALU.is_ge, fill=NEG,
        base=WIN, pattern=[[-1, 256]], channel_multiplier=1),
      reads=[r_const], writes=[r_const])
    b1_stage = None
    ident32 = None
    b1_stage = scr_f32
    ident32 = scr_f32
    E("gpsimd", lambda e: e.memset(ident32[0:128, 256:384], 0.0), writes=[r_scr32])
    E("gpsimd", lambda e: e.affine_select(out=ident32[0:128, 256:384],
        in_=ident32[0:128, 256:384],
        compare_op=ALU.not_equal, fill=1.0, base=0, pattern=[[-1, 128]],
        channel_multiplier=1), reads=[r_scr32], writes=[r_scr32])
    E("sync", lambda e: e.dma_start(out=b1_stage[0:32, 0:128],
        in_=b1_d.rearrange("(c p) -> c p", p=128)), writes=[r_scr32], dma=True)
    E("tensor", lambda e: e.transpose(out=bank[5][:, 0:32], in_=b1_stage[0:32, 0:128],
        identity=ident32[0:32, 256:288]), reads=[r_scr32], writes=[r_bank[5]], tag="b1tr")
    E("vector", lambda e: e.tensor_copy(out=b1_sb[:], in_=bank[5][:, 0:32]),
      reads=[r_bank[5]], writes=[r_const])
    E("sync", lambda e: e.dma_start(out=b2row[0:1, B2OFF:B2OFF+D],
        in_=b2_d.unsqueeze(0)), writes=[r_x[0]], dma=True)
    E("sync", lambda e: e.dma_start(out=borow[0:1, BOOFF:BOOFF+D],
        in_=bo_d.unsqueeze(0)), writes=[r_x[0]], dma=True)

    for hh in range(2):
        E("tensor", lambda e, hh=hh: e.matmul(out=bank[6 + hh][:], lhsT=onesr[:],
            rhs=b2row[0:1, B2OFF+hh*512:B2OFF+(hh+1)*512], start=True, stop=True),
          reads=[r_x[0], r_const], writes=[r_bank[6 + hh]], tag="b2mm")
    for hh in range(2):
        E("vector", lambda e, hh=hh: e.tensor_scalar(out=b2bc[:, hh*512:(hh+1)*512],
            in0=bank[6 + hh][:], scalar1=0.5, scalar2=None, op0=ALU.mult),
          reads=[r_bank[6 + hh]], writes=[r_b2bc])
    for hh in range(2):
        E("tensor", lambda e, hh=hh: e.matmul(out=bank[6 + hh][:], lhsT=onesr[:],
            rhs=borow[0:1, BOOFF+hh*512:BOOFF+(hh+1)*512], start=True, stop=True),
          reads=[r_x[0], r_const], writes=[r_bank[6 + hh]], tag="bomm")
    for hh in range(2):
        E("vector", lambda e, hh=hh: e.tensor_copy(out=bobc[:, hh*512:(hh+1)*512],
            in_=bank[6 + hh][:]), reads=[r_bank[6 + hh]], writes=[r_bobc])

    def prep_matrix(w_dram, n_chunks, chunk_f, r_scrA, scrA, scale_half, tag, alt=True):
        per_stage = (GRP * D) // chunk_f
        n_stages = n_chunks // per_stage
        wv_ = w_dram.rearrange("(c p) f -> p c f", p=128)
        def stage_buf(s):
            i = (s % 2) if alt else 1
            return (x_tm[i], r_x[i])
        for s in range(n_stages):
            xb_, rxb_ = stage_buf(s)
            E("sync", lambda e, s=s, xb_=xb_: e.dma_start(
                out=xb_[:].rearrange("p (c f) -> p c f", c=per_stage),
                in_=wv_[:, s*per_stage:(s+1)*per_stage, :]),
              writes=[rxb_], dma=True, tag=f"{tag}ld{s}")
            for cc in range(per_stage):
                c = s * per_stage + cc
                jout, jres = (junk1, r_junk) if chunk_f == 1024 else (y_fm[1], r_yfm[1])
                E("vector",
                  lambda e, cc=cc, c=c, jout=jout, chunk_f=chunk_f, xb_=xb_: e.scalar_tensor_tensor(
                    out=jout[:, 0:chunk_f],
                    in0=xb_[:, cc*chunk_f:(cc+1)*chunk_f],
                    scalar=-1.0, in1=xb_[:, cc*chunk_f:(cc+1)*chunk_f],
                    op0=ALU.mult, op1=ALU.max, accum_out=sums[:, c:c+1]),
                  reads=[rxb_], writes=[jres, r_sums], tag=f"{tag}abs{c}")
        E("vector", lambda e, n_chunks=n_chunks: e.tensor_reduce(out=colsum[:],
            in_=sums[:, 0:n_chunks], axis=AX.X, op=ALU.add),
          reads=[r_sums], writes=[r_colsum])
        E("tensor", lambda e: e.matmul(out=bank[0][0:1, 0:1], lhsT=colsum[:],
            rhs=onescol[:], start=True, stop=True),
          reads=[r_colsum, r_const], writes=[r_bank[0]], tag=f"{tag}tot")
        n_el = n_chunks * 128 * chunk_f
        E("vector", lambda e, n_el=n_el: e.tensor_scalar(out=thrT[0:1, 0:1],
            in0=bank[0][0:1, 0:1], scalar1=0.5 / n_el, scalar2=0.5e-8,
            op0=ALU.mult, op1=ALU.max),
          reads=[r_bank[0]], writes=[r_thrT])
        E("vector", lambda e: e.tensor_scalar(out=thrT[0:1, 1:2], in0=thrT[0:1, 0:1],
            scalar1=-1.0, scalar2=None, op0=ALU.mult),
          reads=[r_thrT], writes=[r_thrT])
        E("tensor", lambda e: e.matmul(out=bank[0][:, 0:2], lhsT=onesr[:],
            rhs=thrT[:], start=True, stop=True),
          reads=[r_thrT, r_const], writes=[r_bank[0]], tag=f"{tag}bc")
        E("vector", lambda e: e.tensor_copy(out=thrv[:], in_=bank[0][:, 0:2]),
          reads=[r_bank[0]], writes=[r_thrv])
        val = 0.5 if scale_half else 1.0
        for s in range(n_stages):
            xb_, rxb_ = stage_buf(s)
            E("sync", lambda e, s=s, xb_=xb_: e.dma_start(
                out=xb_[:].rearrange("p (c f) -> p c f", c=per_stage),
                in_=wv_[:, s*per_stage:(s+1)*per_stage, :]),
              writes=[rxb_], dma=True, tag=f"{tag}ld2_{s}")
            for cc in range(per_stage):
                c = s * per_stage + cc
                if chunk_f == 1024:
                    if c % 2 == 0:
                        ut, vt, wt = q_loc[:, 0:1024], k_loc[0][:, 0:1024], v_loc[0][:, 0:1024]
                        ru, rv, rw = r_qloc, r_kloc[0], r_vloc[0]
                    else:
                        ut, vt, wt = k_loc[1][:, 0:1024], v_loc[1][:, 0:1024], qpp[:, 0:1024]
                        ru, rv, rw = r_kloc[1], r_vloc[1], r_qpp
                else:
                    ut, vt, wt = g_sb[:, 0:4096], g_sb[:, 4096:8192], g_sb[:, 8192:12288]
                    ru = rv = rw = r_g
                E("vector", lambda e, cc=cc, ut=ut, chunk_f=chunk_f, xb_=xb_: e.tensor_scalar(
                    out=ut, in0=xb_[:, cc*chunk_f:(cc+1)*chunk_f],
                    scalar1=thrv[:, 0:1], scalar2=val, op0=ALU.is_gt, op1=ALU.mult),
                  reads=[rxb_, r_thrv], writes=[ru], tag=f"{tag}u{c}")
                E("vector", lambda e, cc=cc, vt=vt, chunk_f=chunk_f, xb_=xb_: e.tensor_scalar(
                    out=vt, in0=xb_[:, cc*chunk_f:(cc+1)*chunk_f],
                    scalar1=thrv[:, 1:2], scalar2=val, op0=ALU.is_lt, op1=ALU.mult),
                  reads=[rxb_, r_thrv], writes=[rv], tag=f"{tag}v{c}")
                E("vector", lambda e, ut=ut, vt=vt, wt=wt: e.tensor_tensor(
                    out=wt, in0=ut, in1=vt, op=ALU.subtract),
                  reads=[ru, rv], writes=[rw], tag=f"{tag}t{c}")
                if chunk_f == 1024:
                    tb = 2 + (c % 2)
                    tbbf = bank[tb][:].bitcast(BF16)
                    for k in range(8):
                        E("tensor", lambda e, k=k, wt=wt, tbbf=tbbf: e.transpose(
                            out=tbbf[:, k*128:(k+1)*128],
                            in_=wt[:, k*128:(k+1)*128], identity=ident[:]),
                          reads=[rw, r_const], writes=[r_bank[tb]], tag=f"{tag}tr{c}")
                    E("scalar", lambda e, c=c, tbbf=tbbf: e.copy(
                        out=kp_tm[:, (c % 2)*1024:(c % 2)*1024 + 1024], in_=tbbf[:]),
                      reads=[r_bank[tb]], writes=[r_kp], tag=f"{tag}ev{c}")
                    if c % 2 == 1:
                        E("scalar", lambda e, c=c: e.dma_start(
                            out=scr_w1T[c-1:c+1].rearrange("s p k f -> p s (k f)"),
                            in_=kp_tm[:, 0:2048].rearrange("p (s f) -> p s f", s=2)),
                          reads=[r_kp], writes=[r_scrT1], dma=True, tag=f"{tag}sl{c}")
                else:
                    for q4 in range(4):
                        tb = 2 + (q4 % 2)
                        tbbf = bank[tb][:].bitcast(BF16)
                        for k in range(8):
                            j = q4 * 8 + k
                            E("tensor", lambda e, j=j, k=k, wt=wt, tbbf=tbbf: e.transpose(
                                out=tbbf[:, k*128:(k+1)*128],
                                in_=wt[:, j*128:(j+1)*128], identity=ident[:]),
                              reads=[rw, r_const], writes=[r_bank[tb]], tag=f"{tag}tr{c}_{j}")
                        E("scalar", lambda e, q4=q4, tbbf=tbbf: e.copy(
                            out=g_sb[:, 12288 + q4*1024: 12288 + (q4+1)*1024], in_=tbbf[:]),
                          reads=[r_bank[tb]], writes=[r_gc[2]], tag=f"{tag}ev{c}_{q4}")
                    for q4 in range(4):
                        E("scalar", lambda e, c=c, q4=q4: e.dma_start(
                            out=scr_w2T[q4*8:(q4+1)*8, :, c*128:(c+1)*128]
                                .rearrange("fc p d -> p fc d"),
                            in_=g_sb[:, 12288 + q4*1024: 12288 + (q4+1)*1024]
                                .rearrange("p (fc d) -> p fc d", fc=8)),
                          reads=[r_gc[2], r_g], writes=[r_scrT2], dma=True, tag=f"{tag}st{c}_{q4}")

    prep_matrix(w1_d, 32, 1024, r_scrA1, scr_w1t, False, "w1")

    x_view = x_in.rearrange("(n g p) d -> n p g d", p=128, g=GRP)
    out_view = out_d.rearrange("(n g p) d -> n p g d", p=128, g=GRP)

    def load_x(t):
        E("sync", lambda e, t=t: e.dma_start(
            out=x_tm[t % 2][:].rearrange("p (g d) -> p g d", g=GRP),
            in_=x_view[t]), writes=[r_x[t % 2]], dma=True, tag=f"ldx{t}")

    def store_x(t):
        op = E("sync", lambda e, t=t: e.dma_start(
            out=out_view[t],
            in_=x_tm[t % 2][:].rearrange("p (g d) -> p g d", g=GRP)),
            reads=[r_x[t % 2]], writes=[r_out], dma=True, tag=f"stx{t}")
        p.final_wait([op])

    def emit_ln_stats_groups(rx, xt, groups):
        for g in groups:
            for hh in range(2):
                E("vector", lambda e, g=g, hh=hh: e.bn_stats(
                    out=stats[:, (g*2+hh)*6:(g*2+hh)*6+6],
                    in_=xt[:, g*D + hh*512: g*D + (hh+1)*512]),
                  reads=[rx], writes=[r_stats], tag="bnst")
            E("vector", lambda e, g=g: e.bn_aggr(out=mv[:, 2*g:2*g+2],
                in_=stats[:, g*12:(g+1)*12]),
              reads=[r_stats], writes=[r_mv], tag="bnag")

    def emit_ln_finalize(rx, xt):
        mview = mv[:].rearrange("p (g two) -> p g two", two=2)
        E("scalar", lambda e, mview=mview: e.activation(out=rstd[:].unsqueeze(2),
            in_=mview[:, :, 1:2], func=AF.Ln, bias=eps_ln[:, 0:1], scale=1.0),
          reads=[r_mv, r_const], writes=[r_rstd], tag="ln_ln")
        E("scalar", lambda e: e.activation(out=rstd[:], in_=rstd[:],
            func=AF.Exp, scale=-0.5), reads=[r_rstd], writes=[r_rstd], tag="ln_exp")
        for g in range(GRP):
            E("vector", lambda e, g=g: e.tensor_scalar(
                out=y_tm[:, g*D:(g+1)*D], in0=xt[:, g*D:(g+1)*D],
                scalar1=mv[:, 2*g:2*g+1], scalar2=rstd[:, g:g+1],
                op0=ALU.subtract, op1=ALU.mult),
              reads=[rx, r_mv, r_rstd], writes=[r_ytm], tag="lnap")

    def emit_ln_stats(rx, xt, yp):
        emit_ln_stats_groups(rx, xt, range(GRP))
        emit_ln_finalize(rx, xt)

    def emit_ln_transposes(yp):
        for g in range(GRP):
            b = g % 2
            bbf = bank[b][:].bitcast(BF16)
            for c in range(8):
                E("tensor", lambda e, g=g, c=c, bbf=bbf: e.transpose(
                    out=bbf[:, c*128:(c+1)*128],
                    in_=y_tm[:, g*D + c*128: g*D + (c+1)*128], identity=ident[:]),
                  reads=[r_ytm, r_const], writes=[r_bank[b]], tag="ytr")
            E("vector", lambda e, g=g, bbf=bbf, yp=yp: e.tensor_copy(
                out=y_fm[yp][:].rearrange("p (c t) -> p c t", c=8)[:, :, g*128:(g+1)*128],
                in_=bbf[:].rearrange("p (c t) -> p c t", c=8)),
              reads=[r_bank[b]], writes=[r_yfm[yp]], tag="ytre")

    def emit_ln(rx, xt, yp):
        emit_ln_stats(rx, xt, yp)
        emit_ln_transposes(yp)

    ffn_seeded = [False]

    def w1_slab_dma(q_):
        E("sync", lambda e, q_=q_: e.dma_start(
            out=w1s[q_ % 2][:].rearrange("p (s k f) -> p s k f", s=4, k=8),
            in_=scr_w1T[4*q_:4*q_+4].rearrange("s p k f -> p s k f")),
          reads=[r_scrT1], writes=[r_w1s[q_ % 2]], dma=True, tag=f"w1sl{q_}")

    def w2_pair_dma(pr_, buf):
        E("scalar", lambda e, pr_=pr_, buf=buf: e.dma_start(
            out=w2s[buf][:].rearrange("p (s d) -> p s d", s=2),
            in_=scr_w2T[2*pr_:2*pr_+2].rearrange("s p d -> p s d")),
          reads=[r_scrT2], writes=[r_w2s[buf]], dma=True, tag=f"w2ch{pr_}")

    def emit_ffn_seed():
        if not ffn_seeded[0]:
            w1_slab_dma(0)
            w2_pair_dma(0, 0)
            w2_pair_dma(1, 1)
            ffn_seeded[0] = True

    def emit_ffn(t, yp, next_ln=None):
        rx, xt = r_x[t % 2], x_tm[t % 2]

        def w1_round(f_c):
            b = 6 + f_c % 2
            buf, sl = (f_c // 4) % 2, f_c % 4
            for k in range(8):
                E("tensor", lambda e, k=k, b=b, buf=buf, sl=sl, yp=yp: e.matmul(
                    out=bank[b][:], lhsT=w1s[buf][:, sl*1024 + k*128: sl*1024 + (k+1)*128],
                    rhs=y_fm[yp][:, k*512:(k+1)*512], start=(k == 0), stop=(k == 7)),
                  reads=[r_w1s[buf], r_yfm[yp]], writes=[r_bank[b]], tag="w1mm")
            if f_c % 4 == 3 and f_c // 4 + 2 < 8:
                w1_slab_dma(f_c // 4 + 2)
            E("scalar", lambda e, f_c=f_c, b=b: e.activation(
                out=g_sb[:, f_c*512:(f_c+1)*512], in_=bank[b][:], func=AF.Gelu,
                bias=b1_sb[:, f_c:f_c+1], scale=1.0),
              reads=[r_bank[b], r_const], writes=[r_gc[f_c], r_g], tag="gelu")

        def w2_round(f_c, grps, buf=None):
            sl = f_c % 2
            if buf is None:
                buf = (f_c // 2) % 3
            for grp in grps:
                gb = [0, 2, 4, 6][grp]
                for hh in range(2):
                    E("tensor", lambda e, f_c=f_c, grp=grp, hh=hh, buf=buf, sl=sl, gb=gb: e.matmul(
                        out=bank[gb+hh][:],
                        lhsT=g_sb[:, f_c*512 + grp*128: f_c*512 + (grp+1)*128],
                        rhs=w2s[buf][:, sl*1024 + hh*512: sl*1024 + (hh+1)*512],
                        start=(f_c == 0), stop=False),
                      reads=[r_gc[f_c], r_w2s[buf]], writes=[r_bank[gb+hh]], tag="w2mm")
            if f_c % 2 == 1 and f_c // 2 + 3 < 16:
                w2_pair_dma(f_c // 2 + 3, (f_c // 2 + 3) % 3)

        def b2add_evac(grp):
            gb = [0, 2, 4, 6][grp]
            for hh in range(2):
                E("tensor", lambda e, hh=hh, gb=gb: e.matmul(
                    out=bank[gb+hh][:], lhsT=onesd[:],
                    rhs=b2bc[:, hh*512:(hh+1)*512], start=False, stop=True),
                  reads=[r_const, r_b2bc], writes=[r_bank[gb+hh]], tag="b2add")
            for hh in range(2):
                E("vector", lambda e, grp=grp, hh=hh, gb=gb: e.tensor_tensor(
                    out=xt[:, grp*D + hh*512: grp*D + (hh+1)*512],
                    in0=bank[gb+hh][:],
                    in1=xt[:, grp*D + hh*512: grp*D + (hh+1)*512], op=ALU.add),
                  reads=[r_bank[gb+hh]], writes=[rx], tag="ffnres")

        emit_ffn_seed()
        w1_slab_dma(1)
        w2_pair_dma(2, 2)
        w1_round(0)
        for f_c in range(1, 32):
            w1_round(f_c)
            w2_round(f_c - 1, (0, 1, 2))
        w2_round(31, (0, 1, 2))
        for grp in (0, 1, 2):
            b2add_evac(grp)
        if next_ln is not None:
            emit_ln_stats_groups(next_ln[0], next_ln[1], (0, 1, 2))
        for pr_ in range(3):
            w2_pair_dma(pr_, pr_ % 3)
        for f_c in range(32):
            w2_round(f_c, (3,))
        b2add_evac(3)
        ffn_seeded[0] = False
        if next_ln is not None:
            emit_ln_stats_groups(next_ln[0], next_ln[1], (3,))
            emit_ln_finalize(next_ln[0], next_ln[1])
            emit_ln_transposes(next_ln[2])

    def emit_attn(t, yp):
        rx, xt = r_x[t % 2], x_tm[t % 2]
        kb = t % 2

        def w_dma(buf, wd, tag):
            E("gpsimd", lambda e, buf=buf, wd=wd: e.dma_start(
                out=wfull[buf][:].rearrange("p (c d) -> p c d", c=8),
                in_=wd.rearrange("(c p) d -> p c d", p=128)),
              writes=[r_wfull[buf]], dma=True, tag=tag)

        w_dma(0, wq_d, f"wq{t}")
        w_dma(1, wk_d, f"wk{t}")

        for j in range(2):
            for d_c in range(8):
                E("tensor", lambda e, j=j, d_c=d_c, yp=yp: e.matmul(
                    out=bank[j][:], lhsT=wfull[0][:, d_c*D + j*128: d_c*D + (j+1)*128],
                    rhs=y_fm[yp][:, d_c*512:(d_c+1)*512], start=(d_c == 0), stop=(d_c == 7)),
                  reads=[r_wfull[0], r_yfm[yp]], writes=[r_bank[j]], tag="qlocmm")
            E("scalar", lambda e, j=j: e.activation(
                out=q_loc[:, j*512:(j+1)*512], in_=bank[j][:], func=AF.Copy,
                scale=0.125), reads=[r_bank[j]], writes=[r_qloc], tag="qloce")
        for grp in range(GRP):
            for hh in range(2):
                n0, n1 = (0, 512) if hh == 0 else (512, 768)
                for d_c in range(8):
                    E("tensor", lambda e, grp=grp, d_c=d_c, hh=hh, n0=n0, n1=n1, yp=yp: e.matmul(
                        out=bank[2+hh][:, 0:n1-n0],
                        lhsT=y_fm[yp][:, d_c*512 + grp*128: d_c*512 + (grp+1)*128],
                        rhs=wfull[0][:, d_c*D + 256 + n0: d_c*D + 256 + n1],
                        start=(d_c == 0), stop=(d_c == 7)),
                      reads=[r_yfm[yp], r_wfull[0]], writes=[r_bank[2+hh]], tag="qlinmm")
            E("scalar", lambda e: e.copy(out=qf32[:, 0:512], in_=bank[2][:]),
              reads=[r_bank[2]], writes=[r_qf32], tag="qle1")
            E("scalar", lambda e: e.copy(out=qf32[:, 512:768],
                in_=bank[3][:, 0:256]), reads=[r_bank[3]], writes=[r_qf32], tag="qle2")
            E("vector", lambda e: e.tensor_reduce(out=segmax[:], in_=qf32[:, 0:768],
                axis=AX.X, op=ALU.max, negate=True),
              reads=[r_qf32], writes=[r_segmax], tag="qmax")
            E("scalar", lambda e, grp=grp: e.activation(
                out=qexp[:, grp*768:(grp+1)*768], in_=qf32[:, 0:768], func=AF.Exp,
                bias=segmax[:, 0:1], scale=1.0),
              reads=[r_qf32, r_segmax], writes=[r_qexp], tag="qexp")
            E("vector", lambda e, grp=grp: e.tensor_reduce(
                out=segsum[:],
                in_=qexp[:, grp*768:(grp+1)*768].rearrange("p (s x) -> p s x", s=12),
                axis=AX.X, op=ALU.add),
              reads=[r_qexp], writes=[r_segsum], tag="qsum")
            E("vector", lambda e: e.reciprocal(out=nrec12[:], in_=segsum[:]),
              reads=[r_segsum], writes=[r_nrec12], tag="qrec")
            E("vector", lambda e, grp=grp: e.tensor_tensor(
                out=qexp[:, grp*768:(grp+1)*768].rearrange("p (s x) -> p s x", s=12),
                in0=qexp[:, grp*768:(grp+1)*768].rearrange("p (s x) -> p s x", s=12),
                in1=nrec12[:].unsqueeze(2).broadcast_to((128, 12, 64)),
                op=ALU.mult),
              reads=[r_qexp, r_nrec12], writes=[r_qexp], tag="qnrm")
        w_dma(0, wv_d, f"wv{t}")
        for grp in range(GRP):
            b = 4 + grp % 2
            bbf = bank[b][:].bitcast(BF16)
            for pr in range(PAIRS):
                E("tensor", lambda e, grp=grp, pr=pr, bbf=bbf: e.transpose(
                    out=bbf[:, pr*128:(pr+1)*128],
                    in_=qexp[:, grp*768 + pr*128: grp*768 + (pr+1)*128],
                    identity=ident[:]),
                  reads=[r_qexp, r_const], writes=[r_bank[b]], tag="qtr")
            E("vector", lambda e, grp=grp, bbf=bbf: e.tensor_copy(
                out=qpp[:].rearrange("p (pr t) -> p pr t", pr=PAIRS)[:, :, grp*128:(grp+1)*128],
                in_=bbf[:, 0:768].rearrange("p (pr t) -> p pr t", pr=PAIRS)),
              reads=[r_bank[b]], writes=[r_qpp], tag="qtre")

        for j in range(2):
            for d_c in range(8):
                E("tensor", lambda e, j=j, d_c=d_c, yp=yp: e.matmul(
                    out=bank[j][:], lhsT=wfull[1][:, d_c*D + j*128: d_c*D + (j+1)*128],
                    rhs=y_fm[yp][:, d_c*512:(d_c+1)*512], start=(d_c == 0), stop=(d_c == 7)),
                  reads=[r_wfull[1], r_yfm[yp]], writes=[r_bank[j]], tag="klocmm")
            E("scalar", lambda e, j=j: e.activation(
                out=k_loc[kb][:, j*512:(j+1)*512], in_=bank[j][:], func=AF.Copy),
              reads=[r_bank[j]], writes=[r_kloc[kb]], tag="kloce")
        for grp in range(GRP):
            for hh in range(2):
                n0, n1 = (0, 512) if hh == 0 else (512, 768)
                for d_c in range(8):
                    E("tensor", lambda e, grp=grp, d_c=d_c, hh=hh, n0=n0, n1=n1, yp=yp: e.matmul(
                        out=bank[2+hh][:, 0:n1-n0],
                        lhsT=y_fm[yp][:, d_c*512 + grp*128: d_c*512 + (grp+1)*128],
                        rhs=wfull[1][:, d_c*D + 256 + n0: d_c*D + 256 + n1],
                        start=(d_c == 0), stop=(d_c == 7)),
                      reads=[r_yfm[yp], r_wfull[1]], writes=[r_bank[2+hh]], tag="klinmm")
            for hh in range(2):
                n0, n1 = (0, 512) if hh == 0 else (512, 768)
                E("vector", lambda e, hh=hh, n0=n0, n1=n1: e.tensor_scalar(
                    out=elu_a[:, n0:n1], in0=bank[2+hh][:, 0:n1-n0],
                    scalar1=0.0, scalar2=None, op0=ALU.min),
                  reads=[r_bank[2+hh]], writes=[r_elua], tag="elmin")
                E("vector", lambda e, hh=hh, n0=n0, n1=n1: e.tensor_scalar(
                    out=elu_m[:, n0:n1], in0=bank[2+hh][:, 0:n1-n0],
                    scalar1=0.0, scalar2=None, op0=ALU.max),
                  reads=[r_bank[2+hh]], writes=[r_elum], tag="elmax")
            E("scalar", lambda e: e.activation(out=elu_e[:], in_=elu_a[:, 0:768],
                func=AF.Exp), reads=[r_elua], writes=[r_elue], tag="elexp")
            E("vector", lambda e, grp=grp: e.tensor_tensor(
                out=kp_tm[:, grp*768:(grp+1)*768], in0=elu_m[:], in1=elu_e[:],
                op=ALU.add), reads=[r_elum, r_elue], writes=[r_kp], tag="eladd")

        w_dma(1, wo_d, f"wo{t}")

        def v_group(grp):
            b0 = 4 + (grp % 2) * 2
            b1b = b0 + 1
            for hh in range(2):
                b = b0 if hh == 0 else b1b
                for d_c in range(8):
                    E("tensor", lambda e, grp=grp, d_c=d_c, hh=hh, b=b, yp=yp: e.matmul(
                        out=bank[b][:],
                        lhsT=y_fm[yp][:, d_c*512 + grp*128: d_c*512 + (grp+1)*128],
                        rhs=wfull[0][:, d_c*D + hh*512: d_c*D + (hh+1)*512],
                        start=(d_c == 0), stop=(d_c == 7)),
                      reads=[r_yfm[yp], r_wfull[0]], writes=[r_bank[b]], tag="vmm")
            E("scalar", lambda e, grp=grp, b0=b0: e.copy(
                out=v_loc[kb][:, grp*256:(grp+1)*256], in_=bank[b0][:, 0:256]),
              reads=[r_bank[b0]], writes=[r_vloc[kb]], tag="vle")
            E("scalar", lambda e, grp=grp, b0=b0: e.copy(
                out=v_lin[:, grp*768: grp*768 + 256], in_=bank[b0][:, 256:512]),
              reads=[r_bank[b0]], writes=[r_vlin], tag="vle2")
            E("scalar", lambda e, grp=grp, b1b=b1b: e.copy(
                out=v_lin[:, grp*768 + 256:(grp+1)*768], in_=bank[b1b][:]),
              reads=[r_bank[b1b]], writes=[r_vlin], tag="vle3")
        v_group(0)
        v_group(1)

        nb = 2
        for grp in range(GRP):
            for pr in range(PAIRS):
                E("tensor", lambda e, grp=grp, pr=pr: e.matmul(
                    out=bank[nb][:, pr*8 + grp*2: pr*8 + grp*2 + 2],
                    lhsT=kp_tm[:, grp*768 + pr*128: grp*768 + (pr+1)*128],
                    rhs=ones2[:], start=(grp == 0 and pr == 0), stop=True,
                    skip_group_check=True),
                  reads=[r_kp, r_const], writes=[r_bank[nb]], tag="nrmm")
        E("vector", lambda e: e.tensor_copy(out=nraw[:], in_=bank[nb][:, 0:48]),
          reads=[r_bank[nb]], writes=[r_nraw], tag="nre")
        E("vector", lambda e: e.tensor_copy(
            out=ncum[:].rearrange("p (pr n) -> p pr n", pr=6)[:, :, 0:1],
            in_=carry[:].unsqueeze(2)),
          reads=[r_carry], writes=[r_ncum], tag="ncar")
        for pr in range(PAIRS):
            E("vector", lambda e, pr=pr: e.tensor_tensor_scan(
                out=ncum[:, pr*9+1:(pr+1)*9], data0=ones8[:],
                data1=nraw[:, pr*8:(pr+1)*8], initial=carry[:, pr:pr+1],
                op0=ALU.mult, op1=ALU.add),
              reads=[r_nraw, r_carry, r_const], writes=[r_ncum], tag="nscan")
        E("vector", lambda e: e.tensor_copy(
            out=carry[:].unsqueeze(2),
            in_=ncum[:].rearrange("p (pr n) -> p pr n", pr=6)[:, :, 8:9]),
          reads=[r_ncum], writes=[r_carry], tag="ncar2")
        E("vector", lambda e: e.tensor_scalar(out=ncr[:], in0=ncum[:],
            scalar1=1e-6, scalar2=None, op0=ALU.add),
          reads=[r_ncum], writes=[r_ncr], tag="nadd")
        E("vector", lambda e: e.reciprocal(out=ncr[:], in_=ncr[:]),
          reads=[r_ncr], writes=[r_ncr], tag="nrec")
        E("vector", lambda e: e.tensor_tensor(
            out=qpp[:].rearrange("p (pr u t) -> p pr u t", pr=PAIRS, u=CPT),
            in0=qpp[:].rearrange("p (pr u t) -> p pr u t", pr=PAIRS, u=CPT),
            in1=ncr[:].rearrange("p (pr n) -> p pr n", pr=PAIRS)[:, :, 0:8]
                .unsqueeze(3).broadcast_to((128, PAIRS, CPT, CH)),
            op=ALU.mult),
          reads=[r_qpp, r_ncr], writes=[r_qpp], tag="qsc")

        mk4 = masked[:].rearrange("p (blk two x) -> p blk two x", two=2, x=256)
        sb_, ceb, aoE, aoO = 3, 4, 5, 6

        def local_bucket(u):
            first_global = (t == 0 and u == 0)
            for h in range(4):
                b = h % 2
                pb = (h % 2) * 64
                col = (h // 2) * 256
                st = (h < 2)
                qsl = q_loc[pb:pb+64, (h//2)*512 + u*128: (h//2)*512 + (u+1)*128]
                if first_global:
                    E("tensor", lambda e, h=h, b=b, pb=pb, col=col, st=st, qsl=qsl: e.matmul(
                        out=bank[b][:, col + 128: col + 256], lhsT=qsl,
                        rhs=k_loc[kb][pb:pb+64, (h//2)*512: (h//2)*512 + 128],
                        start=st, stop=True, skip_group_check=True),
                      reads=[r_qloc, r_kloc[kb]], writes=[r_bank[b]], tag="dots0")
                elif u == 0:
                    E("tensor", lambda e, h=h, b=b, pb=pb, col=col, st=st, qsl=qsl: e.matmul(
                        out=bank[b][:, col: col + 128], lhsT=qsl,
                        rhs=k_loc[1-kb][pb:pb+64, (h//2)*512 + 384: (h//2)*512 + 512],
                        start=st, stop=False, skip_group_check=True),
                      reads=[r_qloc, r_kloc[1-kb]], writes=[r_bank[b]], tag="dotsp")
                    E("tensor", lambda e, h=h, b=b, pb=pb, col=col, qsl=qsl: e.matmul(
                        out=bank[b][:, col + 128: col + 256], lhsT=qsl,
                        rhs=k_loc[kb][pb:pb+64, (h//2)*512: (h//2)*512 + 128],
                        start=False, stop=True, skip_group_check=True),
                      reads=[r_qloc, r_kloc[kb]], writes=[r_bank[b]], tag="dotsc")
                else:
                    E("tensor", lambda e, h=h, u=u, b=b, pb=pb, col=col, st=st, qsl=qsl: e.matmul(
                        out=bank[b][:, col: col + 256], lhsT=qsl,
                        rhs=k_loc[kb][pb:pb+64, (h//2)*512 + (u-1)*128: (h//2)*512 + (u+1)*128],
                        start=st, stop=True, skip_group_check=True),
                      reads=[r_qloc, r_kloc[kb]], writes=[r_bank[b]], tag="dots")
            for par in range(2):
                if first_global:
                    E("vector", lambda e, par=par: e.tensor_tensor(
                        out=mk4[:, :, par, 128:256],
                        in0=bank[par][:].rearrange("p (bb x) -> p bb x", bb=2)[:, :, 128:256],
                        in1=maskadd[:, 128:256].unsqueeze(1).broadcast_to((128, 2, 128)),
                        op=ALU.add),
                      reads=[r_bank[par], r_const], writes=[r_masked], tag="lmsk0")
                else:
                    E("vector", lambda e, par=par: e.tensor_tensor(
                        out=mk4[:, :, par, :],
                        in0=bank[par][:].rearrange("p (bb x) -> p bb x", bb=2),
                        in1=maskadd[:].unsqueeze(1).broadcast_to((128, 2, 256)),
                        op=ALU.add),
                      reads=[r_bank[par], r_const], writes=[r_masked], tag="lmsk")
            vview = (masked[:].rearrange("p (hh x) -> p hh x", hh=4)[:, :, 128:256]
                     if first_global else
                     masked[:].rearrange("p (hh x) -> p hh x", hh=4))
            E("vector", lambda e, vview=vview: e.tensor_reduce(
                out=lmax[:], in_=vview, axis=AX.X, op=ALU.max, negate=True),
              reads=[r_masked], writes=[r_lmax], tag="lmx")
            for h in range(4):
                sl = (slice(h*256 + 128, (h+1)*256) if first_global
                      else slice(h*256, (h+1)*256))
                E("scalar", lambda e, h=h, sl=sl: e.activation(
                    out=probs[:, sl], in_=masked[:, sl], func=AF.Exp,
                    bias=lmax[:, h:h+1], scale=1.0, accum_out=lsum[:, h:h+1]),
                  reads=[r_masked, r_lmax], writes=[r_probs, r_lsum], tag="lexp")
            E("vector", lambda e: e.reciprocal(out=lrec[:], in_=lsum[:]),
              reads=[r_lsum], writes=[r_lrec], tag="lrc")
            E("vector", lambda e: e.tensor_tensor(
                out=probs[:, 0:1024].rearrange("p (h x) -> p h x", h=4),
                in0=probs[:, 0:1024].rearrange("p (h x) -> p h x", h=4),
                in1=lrec[:].unsqueeze(2).broadcast_to((128, 4, 256)),
                op=ALU.mult),
              reads=[r_probs, r_lrec], writes=[r_probs], tag="lnm")
            tbbf = bank[2][:].bitcast(BF16)
            if first_global:
                E("vector", lambda e: e.memset(bank[2][:], 0.0),
                  writes=[r_bank[2]], tag="ptz")
            for h in range(4):
                for kc in range(2):
                    if first_global and kc == 0:
                        continue
                    E("tensor", lambda e, h=h, kc=kc, tbbf=tbbf: e.transpose(
                        out=tbbf[:, (h*2+kc)*128:(h*2+kc+1)*128],
                        in_=probs[:, h*256 + kc*128: h*256 + (kc+1)*128],
                        identity=ident[:]),
                      reads=[r_probs, r_const], writes=[r_bank[2]], tag="ptr")
            E("vector", lambda e, tbbf=tbbf: e.tensor_copy(
                out=probs[:, 1024:2048], in_=tbbf[:]),
              reads=[r_bank[2]], writes=[r_probsT], tag="ptre")
            for h in range(4):
                pb = (h % 2) * 64
                acol = (h // 2) * 128
                for kc in range(2):
                    if first_global and kc == 0:
                        continue
                    if u == 0 and kc == 0:
                        vbuf, vcol = 1 - kb, 3 * 256
                    else:
                        vbuf, vcol = kb, (u - 1 + kc) * 256
                    st = ((kc == 0) or first_global) and h < 2
                    E("tensor", lambda e, h=h, kc=kc, pb=pb, acol=acol, vbuf=vbuf, vcol=vcol, st=st, kcf=(kc == 0 or first_global): e.matmul(
                        out=bank[7][pb:pb+64, acol:acol+128],
                        lhsT=v_loc[vbuf][:, vcol + h*64: vcol + (h+1)*64],
                        rhs=probs[:, 1024 + (h*2+kc)*128: 1024 + (h*2+kc+1)*128],
                        start=st, stop=(kc == 1), skip_group_check=True),
                      reads=[r_vloc[vbuf], r_probsT], writes=[r_bank[7]], tag="avmm")
            E("vector", lambda e, u=u: e.tensor_copy(
                out=attnout[:, u*128:].rearrange("p (j t) -> p j t", j=2)[:, :, 0:128]
                    if False else
                    attnout[:].rearrange("p (j t) -> p j t", j=8)[:, 0:2, u*128:(u+1)*128],
                in_=bank[7][:, 0:256].rearrange("p (j t) -> p j t", j=2)),
              reads=[r_bank[7]], writes=[r_attnout], tag="ave")

        aout6 = attnout[:, 2*512:].rearrange("p (pr t) -> p pr t", pr=6)

        def linear_pair(cp):
            ue, uo = 2*cp, 2*cp + 1
            first = (t == 0 and cp == 0)
            if not first:
                E("scalar", lambda e: e.copy(out=cumbf[:], in_=bank[sb_][:, 0:384]),
                  reads=[r_bank[sb_]], writes=[r_cumbf], tag="cume")
            for h in range(12):
                pb = (h % 2) * 64
                E("tensor", lambda e, h=h, cp=cp, pb=pb: e.matmul(
                    out=bank[ceb][pb:pb+64, (h//2)*64:(h//2)*64+64],
                    lhsT=kp_tm[0:64, cp*768 + h*64: cp*768 + (h+1)*64],
                    rhs=v_lin[0:64, cp*768 + h*64: cp*768 + (h+1)*64],
                    start=(h < 2), stop=True, skip_group_check=True),
                  reads=[r_kp, r_vlin], writes=[r_bank[ceb]], tag="cevmm")
            E("scalar", lambda e: e.copy(out=cevbf[:], in_=bank[ceb][:, 0:384]),
              reads=[r_bank[ceb]], writes=[r_cevbf], tag="ceve")
            if not first:
                for h in range(12):
                    pb = (h % 2) * 64
                    pr = h // 2
                    ab = aoE if h % 2 == 0 else aoO
                    E("tensor", lambda e, h=h, ue=ue, pb=pb, pr=pr, ab=ab: e.matmul(
                        out=bank[ab][pb:pb+64, pr*64:(pr+1)*64],
                        lhsT=cumbf[pb:pb+64, pr*64:(pr+1)*64],
                        rhs=qpp[pb:pb+64, pr*512 + ue*64: pr*512 + (ue+1)*64],
                        start=(h < 2), stop=True, skip_group_check=True),
                      reads=[r_cumbf, r_qpp], writes=[r_bank[ab]], tag="atme")
                for par in range(2):
                    ab = aoE if par == 0 else aoO
                    pb = par * 64
                    E("scalar", lambda e, ue=ue, ab=ab, pb=pb: e.copy(
                        out=aout6[pb:pb+64, :, ue*64:(ue+1)*64],
                        in_=bank[ab][pb:pb+64, 0:384].rearrange("p (pr t) -> p pr t", pr=6)),
                      reads=[r_bank[ab]], writes=[r_attnout], tag="aoe")
            else:
                for pb in (0, 64):
                    E("vector", lambda e, pb=pb: e.memset(
                        aout6[pb:pb+64, :, 0:64], 0.0),
                      writes=[r_attnout], tag="aoz")
            for h in range(12):
                pb = (h % 2) * 64
                pr = h // 2
                ab = aoE if h % 2 == 0 else aoO
                E("tensor", lambda e, h=h, uo=uo, pb=pb, pr=pr, ab=ab, first=first: e.matmul(
                    out=bank[ab][pb:pb+64, pr*64:(pr+1)*64],
                    lhsT=cevbf[pb:pb+64, pr*64:(pr+1)*64],
                    rhs=qpp[pb:pb+64, pr*512 + uo*64: pr*512 + (uo+1)*64],
                    start=(h < 2), stop=first, skip_group_check=True),
                  reads=[r_cevbf, r_qpp], writes=[r_bank[ab]], tag="atmo1")
            if not first:
                for h in range(12):
                    pb = (h % 2) * 64
                    pr = h // 2
                    ab = aoE if h % 2 == 0 else aoO
                    E("tensor", lambda e, h=h, uo=uo, pb=pb, pr=pr, ab=ab: e.matmul(
                        out=bank[ab][pb:pb+64, pr*64:(pr+1)*64],
                        lhsT=cumbf[pb:pb+64, pr*64:(pr+1)*64],
                        rhs=qpp[pb:pb+64, pr*512 + uo*64: pr*512 + (uo+1)*64],
                        start=False, stop=True, skip_group_check=True),
                      reads=[r_cumbf, r_qpp], writes=[r_bank[ab]], tag="atmo2")
            for par in range(2):
                ab = aoE if par == 0 else aoO
                pb = par * 64
                E("scalar", lambda e, uo=uo, ab=ab, pb=pb: e.copy(
                    out=aout6[pb:pb+64, :, uo*64:(uo+1)*64],
                    in_=bank[ab][pb:pb+64, 0:384].rearrange("p (pr t) -> p pr t", pr=6)),
                  reads=[r_bank[ab]], writes=[r_attnout], tag="aoo")
            for h in range(12):
                pb = (h % 2) * 64
                E("tensor", lambda e, h=h, cp=cp, pb=pb, first=first: e.matmul(
                    out=bank[sb_][pb:pb+64, (h//2)*64:(h//2)*64+64],
                    lhsT=kp_tm[:, cp*768 + h*64: cp*768 + (h+1)*64],
                    rhs=v_lin[:, cp*768 + h*64: cp*768 + (h+1)*64],
                    start=(first and h < 2), stop=True, skip_group_check=True),
                  reads=[r_kp, r_vlin], writes=[r_bank[sb_]], tag="stmm")

        if t > 0:
            E("vector", lambda e: e.tensor_copy(out=bank[sb_][:, 0:384],
                in_=statesv[:]), reads=[r_statesv], writes=[r_bank[sb_]], tag="strst")
        local_bucket(0)
        v_group(2)
        linear_pair(0)
        local_bucket(1)
        v_group(3)
        linear_pair(1)
        local_bucket(2)
        linear_pair(2)
        local_bucket(3)
        linear_pair(3)
        E("vector", lambda e: e.tensor_copy(out=statesv[:], in_=bank[sb_][:, 0:384]),
          reads=[r_bank[sb_]], writes=[r_statesv], tag="stsv")

        for grp in range(GRP):
            b0 = 1 if grp % 2 == 0 else 2
            b1b = 6 if grp % 2 == 0 else 7
            for hh in range(2):
                b = b0 if hh == 0 else b1b
                for d_c in range(8):
                    E("tensor", lambda e, grp=grp, d_c=d_c, hh=hh, b=b: e.matmul(
                        out=bank[b][:],
                        lhsT=attnout[:, d_c*512 + grp*128: d_c*512 + (grp+1)*128],
                        rhs=wfull[1][:, d_c*D + hh*512: d_c*D + (hh+1)*512],
                        start=(d_c == 0), stop=False),
                      reads=[r_attnout, r_wfull[1]], writes=[r_bank[b]], tag="womm")
                E("tensor", lambda e, hh=hh, b=b: e.matmul(
                    out=bank[b][:], lhsT=onesd[:], rhs=bobc[:, hh*512:(hh+1)*512],
                    start=False, stop=True),
                  reads=[r_const, r_bobc], writes=[r_bank[b]], tag="boadd")
                E("vector", lambda e, grp=grp, hh=hh, b=b: e.tensor_tensor(
                    out=xt[:, grp*D + hh*512: grp*D + (hh+1)*512], in0=bank[b][:],
                    in1=xt[:, grp*D + hh*512: grp*D + (hh+1)*512], op=ALU.add),
                  reads=[r_bank[b]], writes=[rx], tag="wores")

    load_x(0)
    ln_i = 0
    emit_ln(r_x[0], x_tm[0], ln_i % 2)
    prep_matrix(w2_d, 8, 4096, r_scrA2, scr_w2t, True, "w2", alt=False)

    load_x(1)
    for t in range(NT):
        rx, xt = r_x[t % 2], x_tm[t % 2]
        p1 = ln_i % 2
        if STAGE_ATTN:
            ln_i += 1
            emit_ffn(t, p1, next_ln=(rx, xt, ln_i % 2))
            emit_ffn_seed()
            emit_attn(t, ln_i % 2)
        else:
            emit_ffn(t, p1)
        ln_i += 1
        p3 = ln_i % 2
        emit_ln(rx, xt, p3)
        if t + 1 < NT:
            ln_i += 1
            emit_ln_stats(r_x[(t + 1) % 2], x_tm[(t + 1) % 2], ln_i % 2)
        emit_ffn(t, p3)
        emit_ffn_seed()
        store_x(t)
        if t + 2 < NT:
            load_x(t + 2)
        if t + 1 < NT:
            emit_ln_transposes(ln_i % 2)

    p.finalize_block(verbose=verbose)
    return nc


def kernel(L0_attn_x, L0_attn_Wq, L0_attn_Wk, L0_attn_Wv, L0_attn_Wo, L0_attn_bo,
           L0_ffn_W1, L0_ffn_b1, L0_ffn_W2, L0_ffn_b2):
    nc = build_program()
    xs = np.ascontiguousarray(np.asarray(L0_attn_x, dtype=np.float32))
    w = {
        "wq": np.ascontiguousarray(np.asarray(L0_attn_Wq, np.float32)),
        "wk": np.ascontiguousarray(np.asarray(L0_attn_Wk, np.float32)),
        "wv": np.ascontiguousarray(np.asarray(L0_attn_Wv, np.float32)),
        "wo": np.ascontiguousarray(np.asarray(L0_attn_Wo, np.float32)),
        "bo": np.ascontiguousarray(np.asarray(L0_attn_bo, np.float32)),
        "w1": np.ascontiguousarray(np.asarray(L0_ffn_W1, np.float32)),
        "b1": np.ascontiguousarray(np.asarray(L0_ffn_b1, np.float32)),
        "w2": np.ascontiguousarray(np.asarray(L0_ffn_W2, np.float32)),
        "b2": np.ascontiguousarray(np.asarray(L0_ffn_b2, np.float32)),
    }
    in_maps = [{"x": xs[b], **w} for b in range(8)]
    res = run_bass_kernel_spmd(nc, in_maps, core_ids=list(range(8)))
    return np.stack([res.results[b]["out"] for b in range(8)], axis=0)
